# version 02
# baseline (speedup 1.0000x reference)
from contextlib import ExitStack

import numpy as np
import ml_dtypes

import concourse.bass as bass
import concourse.mybir as mybir
from concourse.bass_utils import run_bass_kernel_spmd

F32 = mybir.dt.float32
BF16 = mybir.dt.bfloat16
AF = mybir.ActivationFunctionType
ALU = mybir.AluOpType
AX = mybir.AxisListType

NCORES = 8
T = 2048
D = 2048
NT = T // 128
ND = D // 128
NH = 16
DH = 128
E = 8
R = 16
CAP = 5120
I1 = 1408
NJ1 = I1 // 128
J1 = 5632
NJS = J1 // 128
EPS = 1e-5

GC_W8 = 0
GC_LZ = 128
GC_LZ2 = 136
GC_EQ1 = 144
GC_EQ2 = 272
GC_CW = 400
GC_RB = 432
GC_G88 = 440
GC_MPREV = 448
GC_MSK = 456


def _consts():
    pos = np.arange(T, dtype=np.float64)
    dims = np.arange(DH // 2, dtype=np.float64)
    ang = np.power(10000.0, -dims / (DH // 2))
    rad = pos[None, :] * ang[:, None]
    cos = np.cos(rad).astype(np.float32)
    sin = np.sin(rad).astype(np.float32)
    cosF = np.concatenate([cos, cos], axis=0)
    sinF = np.concatenate([sin, -sin], axis=0)
    masks = []
    for r in (0, -128, -256, -384):
        dj = np.arange(128)[:, None]
        di = np.arange(512)[None, :]
        masks.append(((r + di - dj) >= 0).astype(np.float32))
    masks = np.stack(masks, 0).reshape(4 * 128, 512)
    e16 = np.zeros((8, 128), np.float32)
    for e in range(8):
        e16[e, e * 16:(e + 1) * 16] = 1.0
    return {
        "c_ident32": np.eye(128, dtype=np.float32),
        "c_identb": np.eye(128, dtype=np.float32).astype(ml_dtypes.bfloat16),
        "c_cosF": cosF.astype(ml_dtypes.bfloat16),
        "c_sinF": sinF.astype(ml_dtypes.bfloat16),
        "c_masks": masks.astype(ml_dtypes.bfloat16),
        "c_e16": e16.astype(ml_dtypes.bfloat16),
        "c_onesrow": np.ones((1, 128), np.float32).astype(ml_dtypes.bfloat16),
        "c_onescol": np.ones((128, 1), np.float32).astype(ml_dtypes.bfloat16),
        "c_ones32": np.ones((1, 128), np.float32),
    }


class P:
    def __init__(self, layers, fused):
        self.layers = layers
        self.fused = fused
        self.nc = bass.Bass("TRN2", target_bir_lowering=False, num_devices=NCORES)
        self.ctx = ExitStack()
        nc = self.nc
        self.sem = {}
        self.cnt = {}
        for name in ("W", "X", "P", "V", "A", "C"):
            self.sem[name] = self.ctx.enter_context(nc.semaphore(name))
            self.cnt[name] = 0
        self.lastwait = {}
        self.d = {}
        self.s = {}
        self._declare_dram()
        self._alloc_perm()

    def inc(self, ins, s, n=1):
        ins.then_inc(self.sem[s], n)
        self.cnt[s] += n
        return (s, self.cnt[s])

    def ev(self, s):
        return (s, self.cnt[s])

    def wait(self, eng, *evs):
        for e in evs:
            if e is None:
                continue
            s, v = e
            if v <= 0:
                continue
            key = (id(eng), s)
            if self.lastwait.get(key, -1) >= v:
                continue
            eng.wait_ge(self.sem[s], v)
            self.lastwait[key] = v

    def dma(self, eng, out, in_, s, *waits):
        if eng is self.nc.sync:
            eng = self.nc.gpsimd
        self.wait(eng, *waits)
        return self.inc(eng.dma_start(out, in_), s, 16)

    def dma_t(self, out, in_, *waits):
        self.wait(self.nc.sync, *waits)
        return self.inc(self.nc.sync.dma_start(out, in_, transpose=True), "X", 16)

    def V(self, ins):
        e = self.inc(ins, "V")
        self.wait(self.nc.vector, e)
        return e

    def A(self, ins):
        e = self.inc(ins, "A")
        self.wait(self.nc.scalar, e)
        return e

    def _declare_dram(self):
        nc, d, layers = self.nc, self.d, self.layers

        def din(name, shape, dt=F32):
            d[name] = nc.dram_tensor(name, list(shape), dt, kind="ExternalInput").ap()

        def dint(name, shape, dt=F32):
            d[name] = nc.dram_tensor(name, list(shape), dt, kind="Internal").ap()

        din("x", (T, D))
        d["xo"] = nc.dram_tensor("xo", [T, D], F32, kind="ExternalOutput").ap()
        din("c_ident32", (128, 128))
        din("c_identb", (128, 128), BF16)
        din("c_onesrow", (1, 128), BF16)
        din("c_onescol", (128, 1), BF16)
        din("c_ones32", (1, 128))
        dint("ybuf", (T, D))
        if 0 in layers:
            din("c_cosF", (128, T), BF16)
            din("c_sinF", (128, T), BF16)
            din("c_masks", (4 * 128, 512), BF16)
            din("L0_attn_w_aq", (NH, D, DH))
            din("L0_attn_w_ak", (NH, D, DH))
            din("L0_attn_w_av", (NH, D, DH))
            din("L0_attn_w_ao", (NH, DH, D))
        if 1 in layers:
            din("c_e16", (8, 128), BF16)
            din("L0_ffn_base_w", (D, D))
            din("L0_ffn_base_b", (D,))
            din("L0_ffn_router_w", (E, D))
            din("L0_ffn_router_b", (E,))
            din("L0_ffn_lora_A", (E, D, R))
            din("L0_ffn_lora_B", (E, R, D))
            dint("wscr0", (D, D), BF16)
        if 2 in layers:
            din("L1_attn_Wa", (64, D))
            din("L1_attn_ba", (64,))
            din("L1_attn_Wb", (64, D))
            din("L1_attn_bb", (64,))
            din("L1_attn_Wout", (D, 64))
            din("L1_attn_bout", (D,))
        if 3 in layers:
            din("c_maskprev", (8, 8))
            din("L1_ffn_router", (D, E))
            din("L1_ffn_w_up_gate", (E, D, 2 * I1))
            din("L1_ffn_w_down", (E, I1, D))
            din("L1_ffn_shared_w_up_gate", (D, 2 * J1))
            din("L1_ffn_shared_w_down", (J1, D))
            dint("hbuf", ((E + 4) * I1, T), BF16)
            dint("cc_in", (8, 1))
            dint("cc_out", (64, 1))
        if self.fused:
            for i in (1, 2, 3):
                dint(f"xres{i}", (T, D))

    def _alloc_perm(self):
        nc, s, ctx = self.nc, self.s, self.ctx

        def al(name, shape, dt):
            s[name] = ctx.enter_context(nc.sbuf_tensor(name, list(shape), dt))

        al("ident32", (128, 128), F32)
        al("identb", (128, 128), BF16)
        al("onesrow", (1, 128), BF16)
        al("onescol", (128, 1), BF16)
        al("ones32", (1, 128), F32)
        al("stats", (128, 16), F32)
        al("gates", (128, 464), F32)
        al("rwT", (128, 16 * 8), F32)
        al("brow", (1, 2048), BF16)
        al("srow", (1, 512), F32)
        al("scr32", (128, 6144), F32)
        al("constb", (128, 6144), BF16)
        s["pt"] = ctx.enter_context(nc.psum_tensor("pt", [128, 2048], F32))
        s["pb4"] = ctx.enter_context(nc.psum_tensor("pb4", [128, 512], F32))
        s["pb5"] = ctx.enter_context(nc.psum_tensor("pb5", [128, 512], F32))
        s["ptf"] = ctx.enter_context(nc.psum_tensor("ptf", [128, 512], F32))
        s["ptb"] = ctx.enter_context(nc.psum_tensor("ptb", [128, 1024], BF16))
        self.xSB = s["scr32"][:, 0:2048]
        self.ySB = s["scr32"][:, 2048:4096]
        self.xnSB = s["scr32"][:, 4096:6144]

    def load_consts(self):
        nc, s, d = self.nc, self.s, self.d
        for name in ("W", "X", "P", "V", "A", "C"):
            nc.sync.sem_clear(self.sem[name])
        nc.all_engine_barrier()
        evs = [self.dma(nc.sync, s["ident32"][:, :], d["c_ident32"], "X"),
               self.dma(nc.sync, s["identb"][:, :], d["c_identb"], "X"),
               self.dma(nc.sync, s["onesrow"][:, :], d["c_onesrow"], "X"),
               self.dma(nc.sync, s["onescol"][:, :], d["c_onescol"], "X"),
               self.dma(nc.sync, s["ones32"][:, :], d["c_ones32"], "X")]
        if 0 in self.layers:
            evs.append(self.dma(nc.sync, s["constb"][:, 0:2048], d["c_cosF"], "X"))
            evs.append(self.dma(nc.sync, s["constb"][:, 2048:4096], d["c_sinF"], "X"))
            for m in range(4):
                evs.append(self.dma(nc.sync,
                                    s["constb"][:, 4096 + m * 512:4096 + (m + 1) * 512],
                                    d["c_masks"][m * 128:(m + 1) * 128, :], "X"))
        for eng in (nc.tensor, nc.vector, nc.scalar):
            self.wait(eng, *evs)

    def bcast_col(self, src11, dst_col):
        nc, s = self.nc, self.s
        self.wait(nc.tensor, self.ev("V"))
        pe = self.inc(nc.tensor.matmul(s["pb4"][:, 0:1], s["ones32"][:, :], src11,
                                       start=True, stop=True), "P")
        self.wait(nc.scalar, pe)
        ae = self.A(nc.scalar.copy(dst_col, s["pb4"][:, 0:1]))
        self.wait(nc.tensor, ae)
        return ae

    def ln_tile(self, src, out):
        nc, s = self.nc, self.s
        vec, act = nc.vector, nc.scalar
        stats = s["stats"]
        nm, ss, rs = stats[:, 0:1], stats[:, 1:2], stats[:, 2:3]
        self.V(vec.reduce_sum(nm, src, axis=AX.X, negate=True))
        self.V(vec.tensor_scalar(nm, nm, 1.0 / D, None, ALU.mult))
        self.wait(act, self.ev("V"), self.ev("P"))
        acc = self.s["gates"][:, GC_LZ:GC_LZ + 4]
        for c in range(4):
            self.A(act.activation(s["pt"][:, c * 512:(c + 1) * 512],
                                    src[:, c * 512:(c + 1) * 512], AF.Square,
                                    bias=nm, scale=1.0,
                                    accum_out=acc[:, c:c + 1]))
        self.wait(vec, self.ev("A"))
        self.V(vec.tensor_add(acc[:, 0:1], acc[:, 0:1], acc[:, 1:2]))
        self.V(vec.tensor_add(acc[:, 2:3], acc[:, 2:3], acc[:, 3:4]))
        self.V(vec.tensor_add(ss, acc[:, 0:1], acc[:, 2:3]))
        self.V(vec.tensor_scalar(rs, ss, 1.0 / D, EPS, ALU.mult, ALU.add))
        self.wait(act, self.ev("V"))
        self.A(act.activation(rs, rs, AF.Ln))
        self.A(act.activation(rs, rs, AF.Exp, scale=-0.5))
        self.wait(vec, self.ev("A"))
        self.V(vec.tensor_scalar(out, src, nm, rs, ALU.add, ALU.mult))
        return self.ev("V")

    def top2_tile(self, L, tt, scale):
        nc, s = self.nc, self.s
        vec, act = nc.vector, nc.scalar
        g = s["gates"]
        eq1 = g[:, GC_EQ1 + tt * 8:GC_EQ1 + tt * 8 + 8]
        eq2 = g[:, GC_EQ2 + tt * 8:GC_EQ2 + tt * 8 + 8]
        cw = g[:, GC_CW + tt * 2:GC_CW + tt * 2 + 2]
        m8 = s["stats"][:, 8:16]
        l12 = s["stats"][:, 4:6]
        Lz = g[:, GC_LZ:GC_LZ + 8]
        Lz2 = g[:, GC_LZ2:GC_LZ2 + 8]
        self.V(vec.max(m8, L))
        self.V(vec.tensor_copy(l12, m8[:, 0:2]))
        self.V(vec.memset(m8[:, 1:8], -1e30))
        self.V(vec.match_replace(Lz, m8, L, -1e30))
        self.V(vec.tensor_tensor(eq1, L, Lz, ALU.not_equal))
        self.V(vec.tensor_copy(m8[:, 0:1], l12[:, 1:2]))
        self.V(vec.match_replace(Lz2, m8, Lz, -1e30))
        self.V(vec.tensor_tensor(eq2, Lz, Lz2, ALU.not_equal))
        self.V(vec.tensor_tensor(cw[:, 0:1], l12[:, 1:2], l12[:, 0:1], ALU.subtract))
        self.wait(act, self.ev("V"))
        self.A(act.activation(cw[:, 0:1], cw[:, 0:1], AF.Exp))
        self.wait(vec, self.ev("A"))
        self.V(vec.tensor_scalar(cw[:, 0:1], cw[:, 0:1], 1.0, None, ALU.add))
        self.V(vec.reciprocal(cw[:, 0:1], cw[:, 0:1]))
        self.V(vec.tensor_scalar(cw[:, 1:2], cw[:, 0:1], -scale, scale,
                                   ALU.mult, ALU.add))
        if scale != 1.0:
            self.V(vec.tensor_scalar(cw[:, 0:1], cw[:, 0:1], scale, None, ALU.mult))

    def router_prep(self, layer):
        nc, s, d = self.nc, self.s, self.d
        rwT = s["rwT"]
        if layer == 1:
            evs = []
            for j in range(ND):
                evs.append(self.dma(nc.sync, rwT[:, j * 8:(j + 1) * 8],
                                    d["L1_ffn_router"][j * 128:(j + 1) * 128, :], "X"))
            self.wait(nc.tensor, *evs)
            return
        tmp = s["scr32"][0:8, 0:2048]
        e = self.dma(nc.gpsimd, tmp, d["L0_ffn_router_w"], "W")
        self.wait(nc.tensor, e)
        for j in range(ND):
            pe = self.inc(nc.tensor.transpose(s["ptf"][:, 0:8],
                                              tmp[:, j * 128:(j + 1) * 128],
                                              s["ident32"][0:8, 0:8]), "P")
            self.wait(nc.scalar, pe)
            ae = self.A(nc.scalar.copy(rwT[:, j * 8:(j + 1) * 8], s["ptf"][:, 0:8]))
            self.wait(nc.tensor, ae)
        eb = self.dma(nc.sync, s["srow"][0:1, 0:8],
                      d["L0_ffn_router_b"].rearrange("(a e) -> a e", a=1), "X")
        self.wait(nc.tensor, eb)
        pe = self.inc(nc.tensor.matmul(s["pb4"][:, 0:8], s["ones32"][:, :], s["srow"][0:1, 0:8],
                                       start=True, stop=True), "P")
        self.wait(nc.scalar, pe)
        ae = self.A(nc.scalar.copy(s["gates"][:, GC_RB:GC_RB + 8], s["pb4"][:, 0:8]))
        self.wait(nc.tensor, ae)

    def pre_ln(self, x_src, router=None, xnRow=None):
        nc, s = self.nc, self.s
        sync, ten, vec, act = nc.sync, nc.tensor, nc.vector, nc.scalar
        xnT = s["xnT"]
        if router is not None:
            self.router_prep(router)
        xnt32 = self.ySB
        for tt in range(NT):
            ex = self.dma(sync, self.xSB, x_src[tt * 128:(tt + 1) * 128, :], "X",
                          self.ev("V"), self.ev("A"), self.ev("P"),
                          self.ev("X") if tt == 0 else None)
            self.wait(vec, ex)
            self.wait(act, ex)
            self.ln_tile(self.xSB, self.xnSB)
            if xnRow is not None:
                self.V(vec.tensor_copy(xnRow[:, tt * 2048:(tt + 1) * 2048], self.xnSB))
            self.wait(ten, self.ev("V"))
            for j in range(ND):
                bank = s["pt"][:, (j % 4) * 512:(j % 4) * 512 + 128]
                if j % 4 == 0:
                    self.wait(ten, self.ev("A"))
                pe = self.inc(ten.transpose(bank, self.xnSB[:, j * 128:(j + 1) * 128],
                                            s["ident32"][:, :]), "P")
                self.wait(act, pe)
                self.A(act.copy(
                    xnT[:, j * 2048 + tt * 128: j * 2048 + tt * 128 + 128], bank))
                if router is not None:
                    self.A(act.copy(xnt32[:, j * 128:(j + 1) * 128], bank))
            if router is not None:
                self.wait(ten, self.ev("A"))
                for j in range(ND):
                    pe = ten.matmul(s["pb4"][:, 0:8], xnt32[:, j * 128:(j + 1) * 128],
                                    s["rwT"][:, j * 8:(j + 1) * 8],
                                    start=(j == 0), stop=(j == ND - 1))
                pe = self.inc(pe, "P")
                self.wait(vec, pe)
                L = s["gates"][:, 0:8] if False else s["stats"][:, 8:16][:, 0:8]
                L = s["gates"][:, GC_MSK:GC_MSK + 8]
                if router == 1:
                    self.V(vec.tensor_copy(L, s["pb4"][:, 0:8]))
                else:
                    self.V(vec.tensor_tensor(L, s["pb4"][:, 0:8],
                                               s["gates"][:, GC_RB:GC_RB + 8], ALU.add))
                self.wait(ten, self.ev("V"))
                self.top2_tile(L, tt, 2.0 if router == 0 else 1.0)
                g = s["gates"]
                eq1 = g[:, GC_EQ1 + tt * 8:GC_EQ1 + tt * 8 + 8]
                eq2 = g[:, GC_EQ2 + tt * 8:GC_EQ2 + tt * 8 + 8]
                cw = g[:, GC_CW + tt * 2:GC_CW + tt * 2 + 2]
                if router == 0:
                    W8 = g[:, tt * 8:(tt + 1) * 8]
                    self.V(vec.tensor_scalar(W8, eq1, cw[:, 0:1], None, ALU.mult))
                    self.V(vec.scalar_tensor_tensor(W8, eq2, cw[:, 1:2], W8,
                                                      ALU.mult, ALU.add))
                    self.wait(ten, self.ev("V"))
                    pe = self.inc(ten.transpose(s["ptf"][0:8, 0:128], W8, s["ident32"][:, :]), "P")
                    self.wait(act, pe)
                    ae = self.A(act.copy(s["g0T"][0:8, tt * 128:(tt + 1) * 128],
                                           s["ptf"][0:8, 0:128]))
                    self.wait(ten, ae)
                else:
                    for eq, rname in ((eq1, "r_oh1"), (eq2, "r_oh2")):
                        self.wait(ten, self.ev("V"))
                        pe = self.inc(ten.transpose(s["ptf"][0:8, 0:128], eq,
                                                    s["ident32"][:, :]), "P")
                        self.wait(vec, pe)
                        ve = self.V(vec.tensor_copy(
                            s[rname][0:8, tt * 128:(tt + 1) * 128],
                            s["ptf"][0:8, 0:128]))
                        self.wait(ten, ve)

    def post_ln(self, x_src, x_dst, y_dram=None, y_tiles=None, tts=None):
        nc = self.nc
        sync, vec, act = nc.sync, nc.vector, nc.scalar
        for tt in (tts if tts is not None else range(NT)):
            if y_dram is not None:
                ey = self.dma(sync, self.ySB, y_dram[tt * 128:(tt + 1) * 128, :], "X",
                              self.ev("V"), self.ev("A"), self.ev("X"))
                ysrc = self.ySB
                self.wait(vec, ey)
                self.wait(act, ey)
            else:
                ysrc = y_tiles(tt)
            ex = self.dma(sync, self.xSB, x_src[tt * 128:(tt + 1) * 128, :], "X",
                          self.ev("V"), self.ev("X") if y_dram is None else None)
            self.ln_tile(ysrc, self.xnSB)
            self.wait(vec, ex)
            self.V(vec.tensor_add(self.xSB, self.xSB, self.xnSB))
            self.dma(sync, x_dst[tt * 128:(tt + 1) * 128, :], self.xSB, "X", self.ev("V"))

    def l0_attn(self):
        nc, s, d = self.nc, self.s, self.d
        sync, gps, ten, vec, act = nc.sync, nc.gpsimd, nc.tensor, nc.vector, nc.scalar
        xnT, o_all, scrb = s["xnT"], s["bbuf"], s["scrb"]
        cosF = s["constb"][:, 0:2048]
        sinF = s["constb"][:, 2048:4096]
        ms = [s["constb"][:, 4096 + m * 512:4096 + (m + 1) * 512] for m in range(4)]
        qT = scrb[:, 0:2048]
        kT = scrb[:, 2048:4096]
        vS = scrb[:, 4096:6144]
        expS = [scrb[:, 6144:6656], scrb[:, 6656:7168]]
        ws = [scrb[:, 7168:9216], scrb[:, 9216:11264]]
        pt, pb4, pb5, ptf = s["pt"], s["pb4"], s["pb5"], s["ptf"]
        stats = s["stats"]
        srow = s["srow"]
        wsrc = [d["L0_attn_w_aq"], d["L0_attn_w_ak"], d["L0_attn_w_av"]]
        slot_free = [None, None]
        load_ev = {}

        def load_w(h, which):
            slot = (3 * h + which) % 2
            src = wsrc[which][h].rearrange("(j p) d -> p j d", p=128)
            dst = ws[slot].rearrange("p (j d) -> p j d", j=ND)
            load_ev[(h, which)] = (slot, self.dma(gps, dst, src, "W",
                                                  slot_free[slot]))

        def proj_rope(h, which, dstT, amax_col):
            slot, wev = load_ev[(h, which)]
            self.wait(ten, wev, self.ev("A"))
            for j in range(ND):
                for c in range(4):
                    pe = ten.matmul(pt[:, c * 512:(c + 1) * 512],
                                    ws[slot][:, j * 128:(j + 1) * 128],
                                    xnT[:, j * 2048 + c * 512: j * 2048 + (c + 1) * 512],
                                    start=(j == 0), stop=(j == ND - 1))
            pev = self.inc(pe, "P")
            slot_free[slot] = pev
            self.wait(act, pev, self.ev("V"))
            for c in range(4):
                self.A(act.copy(vS[:, c * 512:(c + 1) * 512],
                                  pt[:, c * 512:(c + 1) * 512]))
            self.wait(vec, self.ev("A"))
            for c in range(4):
                t1, t2 = expS[0], expS[1]
                cc = slice(c * 512, (c + 1) * 512)
                self.V(vec.tensor_tensor(t1, vS[:, cc], cosF[:, cc], ALU.mult))
                self.V(vec.tensor_tensor(t2[0:64, :], vS[64:128, cc], sinF[64:128, cc],
                                           ALU.mult))
                self.V(vec.tensor_tensor(t2[64:128, :], vS[0:64, cc], sinF[0:64, cc],
                                           ALU.mult))
                self.V(vec.tensor_add(dstT[:, cc], t1, t2))
            self.V(vec.reduce_max(amax_col, dstT, axis=AX.X,
                                    apply_absolute_value=True))
            return self.ev("V")

        load_w(0, 0)
        load_w(0, 1)
        last_ydma = None
        for h in range(NH):
            evq = proj_rope(h, 0, qT, stats[:, 3:4])
            load_w(h, 2)
            evk = proj_rope(h, 1, kT, stats[:, 4:5])
            self.wait(ten, evq, evk)
            for i in range(2):
                pe = self.inc(ten.transpose(ptf[0:1, 0:128], stats[:, 3 + i:4 + i],
                                            s["ident32"][:, :]), "P")
                self.wait(vec, pe)
                ve = self.V(vec.reduce_max(srow[0:1, i:i + 1], ptf[0:1, 0:128],
                                             axis=AX.X))
                self.wait(ten, ve)
            self.V(vec.tensor_tensor(srow[0:1, 2:3], srow[0:1, 0:1], srow[0:1, 1:2],
                                       ALU.mult))
            self.V(vec.tensor_scalar(srow[0:1, 2:3], srow[0:1, 2:3], -1.0, None,
                                       ALU.mult))
            self.bcast_col(srow[0:1, 2:3], stats[:, 7:8])
            negG = stats[:, 7:8]
            slot, wev = load_ev[(h, 2)]
            self.wait(ten, wev)
            for tt in range(NT):
                bank = pt[:, (tt // 4) * 512 + (tt % 4) * 128:
                          (tt // 4) * 512 + (tt % 4) * 128 + 128]
                for j in range(ND):
                    pe = ten.matmul(bank,
                                    xnT[:, j * 2048 + tt * 128: j * 2048 + tt * 128 + 128],
                                    ws[slot][:, j * 128:(j + 1) * 128],
                                    start=(j == 0), stop=(j == ND - 1))
            pev = self.inc(pe, "P")
            slot_free[slot] = pev
            self.wait(act, pev, self.ev("V"))
            for c in range(4):
                self.A(act.copy(vS[:, c * 512:(c + 1) * 512],
                                  pt[:, c * 512:(c + 1) * 512]))
            self.wait(ten, self.ev("A"))
            if h + 1 < NH:
                load_w(h + 1, 0)
                load_w(h + 1, 1)
            for ic in range(4):
                njt = min(NT, (ic + 1) * 4)
                expev = [None, None]
                for jt in range(njt):
                    sbank = (pb4, pb5)[jt % 2]
                    self.wait(ten, expev[jt % 2])
                    pe = self.inc(ten.matmul(sbank[:, :], kT[:, jt * 128:(jt + 1) * 128],
                                             qT[:, ic * 512:(ic + 1) * 512],
                                             start=True, stop=True), "P")
                    self.wait(act, pe)
                    ae = self.A(act.activation(expS[jt % 2], sbank[:, :], AF.Exp,
                                                 bias=negG, scale=1.0 / DH))
                    ready = ae
                    off = ic * 512 - jt * 128
                    if -384 <= off <= 0:
                        mi = (-off) // 128
                        self.wait(vec, ae)
                        ready = self.V(vec.tensor_tensor(expS[jt % 2], expS[jt % 2],
                                                           ms[mi], ALU.mult))
                    self.wait(ten, ready)
                    pe = ten.matmul(pt[:, 0:512], vS[:, jt * 128:(jt + 1) * 128],
                                    expS[jt % 2], start=(jt == 0), stop=(jt == njt - 1))
                    pe = ten.matmul(pt[0:1, 512:1024], s["onescol"][:, :], expS[jt % 2],
                                    start=(jt == 0), stop=(jt == njt - 1))
                    expev[jt % 2] = self.inc(pe, "P")
                pev = self.ev("P")
                self.wait(vec, pev)
                ve = self.V(vec.reciprocal(s["srow"][0:1, 0:512], pt[0:1, 512:1024]))
                self.wait(ten, ve)
                pe = self.inc(ten.matmul(pt[:, 1024:1536], s["ones32"][:, :],
                                         s["srow"][0:1, 0:512], start=True, stop=True), "P")
                self.wait(act, pe)
                ae = self.A(act.copy(expS[0], pt[:, 1024:1536]))
                self.wait(vec, ae)
                ve = self.V(vec.tensor_tensor(
                    o_all[:, h * 2048 + ic * 512: h * 2048 + (ic + 1) * 512],
                    pt[:, 0:512], expS[0], ALU.mult))
                self.wait(ten, ve)
        wstage = scrb[:, 0:8192]
        ystage = self.ySB
        for oc in range(4):
            evs = []
            for h in range(NH):
                evs.append(self.dma(gps, wstage[:, h * 512:(h + 1) * 512],
                                    d["L0_attn_w_ao"][h][:, oc * 512:(oc + 1) * 512],
                                    "W", self.ev("P") if (oc == 0 and h == 0) else None))
            self.wait(ten, *evs)
            for tt in range(NT):
                bank = (pb4, pb5)[tt % 2]
                self.wait(ten, self.ev("A"))
                for hh in range(NH):
                    pe = ten.matmul(bank[:, :],
                                    o_all[:, hh * 2048 + tt * 128: hh * 2048 + tt * 128 + 128],
                                    wstage[:, hh * 512:(hh + 1) * 512],
                                    start=(hh == 0), stop=(hh == NH - 1))
                pe = self.inc(pe, "P")
                self.wait(act, pe, last_ydma)
                ae = self.A(act.copy(ystage[:, 0:512], bank[:, :]))
                last_ydma = self.dma(
                    sync, d["ybuf"][tt * 128:(tt + 1) * 128, oc * 512:(oc + 1) * 512],
                    ystage[:, 0:512], "X", ae)

    def l0_ffn(self):
        nc, s, d = self.nc, self.s, self.d
        sync, gps, ten, vec, act = nc.sync, nc.gpsimd, nc.tensor, nc.vector, nc.scalar
        xnT, baseWT, scrb = s["xnT"], s["bbuf"], s["scrb"]
        pt, pb4, pb5 = s["pt"], s["pb4"], s["pb5"]
        A_all = scrb[:, 0:2048]
        B_all = scrb[:, 2048:4096]
        gxs = scrb[:, 4096:6144]
        gmid = scrb[:, 6144:8192]
        g0Tb = scrb[:, 8192:10240]
        ec = self.dma(gps, d["wscr0"], d["L0_ffn_base_w"], "W", self.ev("P"))
        tev = []
        for j in range(ND):
            tev.append(self.dma_t(baseWT[:, j * 2048:(j + 1) * 2048],
                                  d["wscr0"][:, j * 128:(j + 1) * 128], ec))
        aev = []
        for e in range(E):
            src = d["L0_ffn_lora_A"][e].rearrange("(j p) r -> p j r", p=128)
            dst = A_all.rearrange("p (j er) -> p j er", j=ND)[:, :, e * R:(e + 1) * R]
            aev.append(self.dma(gps, dst, src, "W"))
        bev = self.dma(gps, B_all, d["L0_ffn_lora_B"].rearrange("e r d -> (e r) d"), "W")
        ebb = self.dma(gps, s["brow"][:, :], d["L0_ffn_base_b"].rearrange("(a d) -> a d", a=1), "W")
        ev = self.V(vec.tensor_copy(g0Tb[0:8, :], s["g0T"][0:8, :]))
        self.wait(ten, ev, bev, ebb, *aev)
        ystage = self.ySB
        for c in range(4):
            cc = slice(c * 512, (c + 1) * 512)
            pe = self.inc(ten.matmul(pb4[:, :], s["e16b"][:, :], g0Tb[0:8, cc],
                                     start=True, stop=True), "P")
            self.wait(act, pe)
            ae = self.A(act.copy(gxs[:, cc], pb4[:, :]))
            self.wait(ten, self.ev("A"))
            for j in range(ND):
                pe = ten.matmul(pb5[:, :], A_all[:, j * 128:(j + 1) * 128],
                                xnT[:, j * 2048 + c * 512: j * 2048 + (c + 1) * 512],
                                start=(j == 0), stop=(j == ND - 1))
            pe = self.inc(pe, "P")
            self.wait(vec, pe, ae)
            ve = self.V(vec.tensor_tensor(gmid[:, cc], pb5[:, :], gxs[:, cc],
                                            ALU.mult))
            self.wait(ten, ve)
        self.wait(ten, *tev)
        last_ydma = None
        for tt in range(NT):
            for oc in range(4):
                bank = pt[:, oc * 512:(oc + 1) * 512]
                if oc == 0:
                    self.wait(ten, self.ev("A"))
                for j in range(ND):
                    pe = ten.matmul(bank,
                                    xnT[:, j * 2048 + tt * 128: j * 2048 + tt * 128 + 128],
                                    baseWT[:, j * 2048 + oc * 512: j * 2048 + (oc + 1) * 512],
                                    start=(j == 0), stop=False)
                pe = ten.matmul(bank, s["onesrow"][:, :], s["brow"][0:1, oc * 512:(oc + 1) * 512],
                                start=False, stop=False)
                pe = ten.matmul(bank, gmid[:, tt * 128:(tt + 1) * 128],
                                B_all[:, oc * 512:(oc + 1) * 512], start=False, stop=True)
                pe = self.inc(pe, "P")
                self.wait(act, pe, last_ydma)
                self.A(act.copy(ystage[:, oc * 512:(oc + 1) * 512], bank))
            last_ydma = self.dma(sync, d["ybuf"][tt * 128:(tt + 1) * 128, :], ystage,
                                 "X", self.ev("A"))

    def l1_attn(self):
        nc, s, d = self.nc, self.s, self.d
        sync, gps, ten, vec, act = nc.sync, nc.gpsimd, nc.tensor, nc.vector, nc.scalar
        xnT, xnRow, scrb = s["xnT"], s["bbuf"], s["scrb"]
        pt, pb4, ptb = s["pt"], s["pb4"], s["ptb"]
        aT = s["scr32"][0:64, 0:2048]
        bT = s["scr32"][0:64, 2048:4096]
        hsT = s["scr32"][0:64, 4096:6144]
        WaTb = scrb[:, 0:1024]
        WbTb = scrb[:, 1024:2048]
        WoutTb = scrb[0:64, 2048:4096]
        hsb = scrb[0:64, 4096:6144]
        wtmp = scrb[0:64, 6144:8192]
        wotmp = scrb[:, 8192:9216]
        bcol = s["stats"][0:64, 8:10]
        for (wname, dstT) in (("L1_attn_Wa", WaTb), ("L1_attn_Wb", WbTb)):
            ew = self.dma(gps, wtmp, d[wname], "W", self.ev("P"))
            self.wait(ten, ew)
            for j in range(ND):
                pe = self.inc(ten.transpose(ptb[:, 0:64], wtmp[0:64, j * 128:(j + 1) * 128],
                                            s["identb"][0:64, 0:64]), "P")
                self.wait(act, pe)
                ae = self.A(act.copy(dstT[:, j * 64:(j + 1) * 64], ptb[:, 0:64]))
                self.wait(ten, ae)
        ew = self.dma(gps, wotmp.rearrange("p (j c) -> p j c", j=ND),
                      d["L1_attn_Wout"].rearrange("(j p) c -> p j c", p=128), "W")
        self.wait(ten, ew)
        for j in range(ND):
            pe = self.inc(ten.transpose(ptb[0:64, 0:128], wotmp[:, j * 64:(j + 1) * 64],
                                        s["identb"][:, :]), "P")
            self.wait(act, pe)
            ae = self.A(act.copy(WoutTb[:, j * 128:(j + 1) * 128], ptb[0:64, 0:128]))
            self.wait(ten, ae)
        eba = self.dma(sync, bcol[:, 0:1], d["L1_attn_ba"].rearrange("(p a) -> p a", a=1), "X")
        ebb = self.dma(sync, bcol[:, 1:2], d["L1_attn_bb"].rearrange("(p a) -> p a", a=1), "X")
        ebo = self.dma(gps, s["brow"][:, :], d["L1_attn_bout"].rearrange("(a c) -> a c", a=1), "W")
        self.wait(act, eba, ebb)
        for (wT, bias, dstT) in ((WaTb, bcol[:, 0:1], aT), (WbTb, bcol[:, 1:2], bT)):
            for c in range(4):
                for j in range(ND):
                    pe = ten.matmul(pb4[0:64, :], wT[:, j * 64:(j + 1) * 64],
                                    xnT[:, j * 2048 + c * 512: j * 2048 + (c + 1) * 512],
                                    start=(j == 0), stop=(j == ND - 1))
                pe = self.inc(pe, "P")
                self.wait(act, pe)
                ae = self.A(act.activation(dstT[:, c * 512:(c + 1) * 512], pb4[0:64, :],
                                             AF.Tanh, bias=bias, scale=1.0))
                self.wait(ten, ae)
        self.wait(vec, self.ev("A"))
        self.V(vec.tensor_tensor_scan(hsT, aT, bT, 0.0, ALU.mult, ALU.add))
        sv = self.V(vec.tensor_copy(hsb, hsT))
        self.wait(ten, sv, ebo)
        ystage = self.ySB
        last_ydma = None
        for tt in range(NT):
            self.wait(ten, self.ev("V"))
            for dc in range(4):
                bank = pt[:, dc * 512:(dc + 1) * 512]
                pe = ten.matmul(bank, hsb[:, tt * 128:(tt + 1) * 128],
                                WoutTb[:, dc * 512:(dc + 1) * 512], start=True, stop=False)
                pe = ten.matmul(bank, s["onesrow"][:, :], s["brow"][0:1, dc * 512:(dc + 1) * 512],
                                start=False, stop=True)
                self.inc(pe, "P")
            pev = self.ev("P")
            self.wait(vec, pev, last_ydma)
            ve = self.V(vec.tensor_add(ystage, pt[:, :],
                                         xnRow[:, tt * 2048:(tt + 1) * 2048]))
            last_ydma = self.dma(sync, d["ybuf"][tt * 128:(tt + 1) * 128, :], ystage,
                                 "X", ve)

    def l1_route_finish(self):
        nc, s, d = self.nc, self.s, self.d
        sync, gps, ten, vec, act = nc.sync, nc.gpsimd, nc.tensor, nc.vector, nc.scalar
        oh1 = s["r_oh1"][0:8, :]
        oh2 = s["r_oh2"][0:8, :]
        Ssum = s["r_S"][0:8, :]
        sel = s["r_sel"][0:8, :]
        incl = s["r_incl"][0:8, :]
        ones8 = s["r_ones"][0:8, :]
        g = s["gates"]
        self.V(vec.tensor_add(Ssum, oh1, oh2))
        self.V(vec.memset(ones8, 1.0))
        sv = self.V(vec.tensor_tensor_scan(incl, ones8, Ssum, 0.0, ALU.mult, ALU.add))
        cdma = self.dma(sync, d["cc_in"], incl[:, T - 1:T], "X", sv)
        self.wait(gps, cdma)
        cc = self.inc(gps.collective_compute(
            "AllGather", ALU.bypass, replica_groups=[list(range(NCORES))],
            ins=[d["cc_in"]], outs=[d["cc_out"]], unique_tensors="No"), "C", 1)
        g88 = g[0:8, GC_G88:GC_G88 + 8]
        mprev = g[0:8, GC_MPREV:GC_MPREV + 8]
        emp = self.dma(sync, mprev, d["c_maskprev"], "X")
        edg = self.dma(sync, g88, d["cc_out"].rearrange("(r e) a -> r (e a)", e=8), "X", cc)
        self.wait(ten, edg)
        pe = self.inc(ten.transpose(s["ptf"][0:8, 0:8], g88, s["ident32"][0:8, 0:8]), "P")
        self.wait(vec, pe, emp)
        self.V(vec.tensor_tensor(g[0:8, GC_MSK:GC_MSK + 8], s["ptf"][0:8, 0:8], mprev,
                                   ALU.mult))
        off = s["stats"][0:8, 8:9]
        ve = self.V(vec.reduce_sum(off, g[0:8, GC_MSK:GC_MSK + 8], axis=AX.X))
        self.wait(ten, ve)
        for k, oh in ((0, oh1), (1, oh2)):
            sub = Ssum if k == 0 else oh2
            self.V(vec.tensor_tensor(sel, incl, sub, ALU.subtract))
            self.V(vec.tensor_scalar(sel, sel, off, None, ALU.add))
            self.V(vec.tensor_tensor(sel, sel, oh, ALU.mult))
            for tt in range(NT):
                self.wait(ten, self.ev("V"))
                pe = self.inc(ten.transpose(s["ptf"][:, 0:8],
                                            sel[:, tt * 128:(tt + 1) * 128],
                                            s["ident32"][0:8, 0:8]), "P")
                self.wait(vec, pe)
                pos = s["stats"][:, 10:11]
                self.V(vec.reduce_sum(pos, s["ptf"][:, 0:8], axis=AX.X))
                val = s["stats"][:, 11:12]
                self.V(vec.tensor_scalar(val, pos, float(CAP), None, ALU.is_lt))
                cwk = g[:, GC_CW + tt * 2 + k: GC_CW + tt * 2 + k + 1]
                wk = s["stats"][:, 12 + k:13 + k]
                ve = self.V(vec.tensor_tensor(wk, cwk, val, ALU.mult))
                self.wait(ten, ve)
                eqbase = GC_EQ1 if k == 0 else GC_EQ2
                eq = g[:, eqbase + tt * 8:eqbase + tt * 8 + 8]
                W8 = g[:, tt * 8:(tt + 1) * 8]
                if k == 0:
                    self.V(vec.tensor_scalar(W8, eq, wk, None, ALU.mult))
                else:
                    self.V(vec.scalar_tensor_tensor(W8, eq, wk, W8, ALU.mult, ALU.add))

    def l1_ffn_a(self):
        nc, s, d = self.nc, self.s, self.d
        sync, gps, ten, vec, act = nc.sync, nc.gpsimd, nc.tensor, nc.vector, nc.scalar
        xnT = s["xnT"]
        pb4, pb5 = s["pb4"], s["pb5"]
        wst = [s["pascr"][:, 0:4096], s["pascr"][:, 4096:8192]]
        hrow = s["pascr"][:, 8192:10240]
        sg = s["scr32"][:, 0:512]
        wfree = [None, None]
        hdma = None
        cnt = 0
        for ei in range(E + 1):
            shared = (ei == E)
            nj = NJS if shared else NJ1
            basej = E * NJ1 * 128 if shared else ei * NJ1 * 128
            for j in range(nj):
                slot = cnt % 2
                cnt += 1
                if shared:
                    gsrc = d["L1_ffn_shared_w_up_gate"][:, j * 128:(j + 1) * 128]
                    usrc = d["L1_ffn_shared_w_up_gate"][:, J1 + j * 128: J1 + (j + 1) * 128]
                else:
                    gsrc = d["L1_ffn_w_up_gate"][ei][:, j * 128:(j + 1) * 128]
                    usrc = d["L1_ffn_w_up_gate"][ei][:, I1 + j * 128: I1 + (j + 1) * 128]
                e1 = self.dma(gps, wst[slot][:, 0:2048].rearrange("p (k c) -> p k c", k=ND),
                              gsrc.rearrange("(k p) c -> p k c", p=128), "W", wfree[slot])
                e2 = self.dma(gps, wst[slot][:, 2048:4096].rearrange("p (k c) -> p k c", k=ND),
                              usrc.rearrange("(k p) c -> p k c", p=128), "W")
                self.wait(ten, e1, e2)
                for c in range(4):
                    for jj in range(ND):
                        ten.matmul(pb4[:, :], wst[slot][:, jj * 128:(jj + 1) * 128],
                                   xnT[:, jj * 2048 + c * 512: jj * 2048 + (c + 1) * 512],
                                   start=(jj == 0), stop=(jj == ND - 1))
                        pu = ten.matmul(pb5[:, :],
                                        wst[slot][:, 2048 + jj * 128: 2048 + (jj + 1) * 128],
                                        xnT[:, jj * 2048 + c * 512: jj * 2048 + (c + 1) * 512],
                                        start=(jj == 0), stop=(jj == ND - 1))
                    pev = self.inc(pu, "P")
                    self.wait(act, pev)
                    ae = self.A(act.activation(sg, pb4[:, :], AF.Silu))
                    self.wait(vec, ae, hdma)
                    ve = self.V(vec.tensor_tensor(hrow[:, c * 512:(c + 1) * 512], sg,
                                                    pb5[:, :], ALU.mult))
                    self.wait(ten, ve)
                wfree[slot] = self.ev("P")
                hdma = self.dma(sync, d["hbuf"][basej + j * 128: basej + (j + 1) * 128, :],
                                hrow, "X", self.ev("V"))

    def l1_ffn_b(self):
        nc, s, d = self.nc, self.s, self.d
        sync, gps, ten, vec, act = nc.sync, nc.gpsimd, nc.tensor, nc.vector, nc.scalar
        pt = s["pt"]
        yacc = s["yacc"]
        wdB = s["wdB"]
        hst = [s["hst"][:, 0:1408], s["hst"][:, 1408:2816]]
        g = s["gates"]
        xsrc = d["xres3"] if self.fused else d["x"]
        self.hx_ev = self.ev("X")
        groups = [("r", e) for e in range(E)] + [("s", gi) for gi in range(4)]
        ve = self.V(vec.memset(yacc[:, :], 0.0))
        for half in range(2):
            tts = list(range(half * 8, half * 8 + 8))
            first = (half == 0)
            for kind, gi in groups:
                if kind == "r":
                    wsrc = d["L1_ffn_w_down"][gi]
                    basej = gi * NJ1 * 128
                else:
                    wsrc = d["L1_ffn_shared_w_down"][gi * I1:(gi + 1) * I1, :]
                    basej = (E + gi) * NJ1 * 128
                wev = []
                for j in range(NJ1):
                    wev.append(self.dma(gps, wdB[:, j * 2048:(j + 1) * 2048],
                                        wsrc[j * 128:(j + 1) * 128, :], "W",
                                        None if first else self.ev("P")))
                first = False
                hev = [None, None]
                for ti, tt in enumerate(tts):
                    hs = hst[ti % 2]
                    he = self.dma(sync, hs.rearrange("p (j c) -> p j c", j=NJ1),
                                  d["hbuf"][basej:basej + NJ1 * 128,
                                            tt * 128:(tt + 1) * 128].rearrange(
                                      "(j p) c -> p j c", p=128), "X", hev[ti % 2],
                                  self.hx_ev)
                    self.wait(ten, he, *wev)
                    for dc in range(4):
                        bank = pt[:, dc * 512:(dc + 1) * 512]
                        for j in range(NJ1):
                            pe = ten.matmul(bank, hs[:, j * 128:(j + 1) * 128],
                                            wdB[:, j * 2048 + dc * 512: j * 2048 + (dc + 1) * 512],
                                            start=(j == 0), stop=(j == NJ1 - 1))
                        pe = self.inc(pe, "P")
                        self.wait(vec, pe)
                        ys = yacc[:, ti * 2048 + dc * 512: ti * 2048 + (dc + 1) * 512]
                        if kind == "r":
                            wcol = g[:, tt * 8 + gi: tt * 8 + gi + 1]
                            ve = self.V(vec.scalar_tensor_tensor(
                                ys, bank, wcol, ys, ALU.mult, ALU.add))
                        else:
                            ve = self.V(vec.tensor_add(ys, ys, bank))
                        self.wait(ten, ve)
                    hev[ti % 2] = self.ev("P")
            self.post_ln(xsrc, d["xo"],
                         y_tiles=(lambda h: (lambda tt: yacc[:, (tt - h * 8) * 2048:
                                                             (tt - h * 8 + 1) * 2048]))(half),
                         tts=tts)
            if half == 0:
                self.wait(vec, self.ev("X"))
                self.V(vec.memset(yacc[:, :], 0.0))

    def emit(self):
        nc, s, d = self.nc, self.s, self.d
        self.load_consts()
        layers = self.layers
        fused = self.fused
        xsrc = d["x"]
        with ExitStack() as big:
            s["xnT"] = big.enter_context(nc.sbuf_tensor("xnT", [128, ND * 2048], BF16))
            if 0 in layers or 1 in layers or 2 in layers:
                with ExitStack() as mid:
                    s["bbuf"] = mid.enter_context(
                        nc.sbuf_tensor("bbuf", [128, ND * 2048], BF16))
                    s["scrb"] = mid.enter_context(
                        nc.sbuf_tensor("scrb", [128, 11264], BF16))
                    if 1 in layers:
                        s["g0T"] = mid.enter_context(nc.sbuf_tensor("g0T", [8, 2048], F32))
                        s["e16b"] = mid.enter_context(nc.sbuf_tensor("e16b", [8, 128], BF16))
                        self.wait(nc.tensor,
                                  self.dma(nc.sync, s["e16b"][:, :], d["c_e16"], "X"))
                    if 0 in layers:
                        self.pre_ln(xsrc)
                        self.l0_attn()
                        dst = d["xres1"] if fused else d["xo"]
                        self.post_ln(xsrc, dst, y_dram=d["ybuf"])
                        xsrc = dst
                    if 1 in layers:
                        self.pre_ln(xsrc, router=0)
                        self.l0_ffn()
                        dst = d["xres2"] if fused else d["xo"]
                        self.post_ln(xsrc, dst, y_dram=d["ybuf"])
                        xsrc = dst
                    if 2 in layers:
                        self.pre_ln(xsrc, xnRow=s["bbuf"])
                        self.l1_attn()
                        dst = d["xres3"] if fused else d["xo"]
                        self.post_ln(xsrc, dst, y_dram=d["ybuf"])
                        xsrc = dst
            if 3 in layers:
                with ExitStack() as mid:
                    for rn in ("r_oh1", "r_oh2", "r_S", "r_sel", "r_incl", "r_ones"):
                        s[rn] = mid.enter_context(
                            nc.sbuf_tensor(rn, [8, 2048], F32))
                    self.pre_ln(xsrc, router=1)
                    self.l1_route_finish()
                    with ExitStack() as inner:
                        s["pascr"] = inner.enter_context(
                            nc.sbuf_tensor("pascr", [128, 10240], BF16))
                        self.l1_ffn_a()
        if 3 in layers:
            with ExitStack() as mid:
                s["yacc"] = mid.enter_context(nc.sbuf_tensor("yacc", [128, 8 * 2048], F32))
                s["wdB"] = mid.enter_context(nc.sbuf_tensor("wdB", [128, NJ1 * 2048], BF16))
                s["hst"] = mid.enter_context(nc.sbuf_tensor("hst", [128, 2 * 1408], BF16))
                self.l1_ffn_b()
        nc = self.nc
        nc.vector.wait_ge(self.sem["V"], self.cnt["V"])
        nc.scalar.wait_ge(self.sem["A"], self.cnt["A"])
        nc.tensor.wait_ge(self.sem["P"], self.cnt["P"])
        nc.gpsimd.wait_ge(self.sem["W"], self.cnt["W"])
        if self.cnt["C"]:
            nc.gpsimd.wait_ge(self.sem["C"], self.cnt["C"])
        nc.sync.wait_ge(self.sem["X"], self.cnt["X"])
        nc.all_engine_barrier()
        self.ctx.close()
        return self.nc


_CONSTS = _consts()


def _in_maps(layers, inputs, xs):
    maps = []
    need = {"c_ident32", "c_identb", "c_onesrow", "c_onescol", "c_ones32"}
    wnames = []
    if 0 in layers:
        need |= {"c_cosF", "c_sinF", "c_masks"}
        wnames += ["L0_attn_w_aq", "L0_attn_w_ak", "L0_attn_w_av", "L0_attn_w_ao"]
    if 1 in layers:
        need |= {"c_e16"}
        wnames += ["L0_ffn_base_w", "L0_ffn_base_b", "L0_ffn_router_w",
                   "L0_ffn_router_b", "L0_ffn_lora_A", "L0_ffn_lora_B"]
    if 2 in layers:
        wnames += ["L1_attn_Wa", "L1_attn_ba", "L1_attn_Wb", "L1_attn_bb",
                   "L1_attn_Wout", "L1_attn_bout"]
    if 3 in layers:
        wnames += ["L1_ffn_router", "L1_ffn_w_up_gate", "L1_ffn_w_down",
                   "L1_ffn_shared_w_up_gate", "L1_ffn_shared_w_down"]
    for c in range(NCORES):
        m = {"x": np.ascontiguousarray(xs[c])}
        for n in need:
            m[n] = _CONSTS[n]
        for n in wnames:
            m[n] = inputs[n]
        if 3 in layers:
            mp = np.zeros((8, 8), np.float32)
            mp[:, :c] = 1.0
            m["c_maskprev"] = mp
        maps.append(m)
    return maps


def kernel(**inputs):
    inputs = {k: np.asarray(v) for k, v in inputs.items()}
    x = inputs["L0_attn_x"]
    xs = [np.asarray(x[c]) for c in range(NCORES)]
    layers = [0, 1, 2, 3]
    nc = P(layers, fused=True).emit()
    maps = _in_maps(layers, inputs, xs)
    res = run_bass_kernel_spmd(nc, maps, core_ids=list(range(NCORES)))
    return np.stack([res.results[c]["xo"] for c in range(NCORES)], axis=0)


# version 05
# speedup vs baseline: 1.2862x; 1.2862x over previous
from contextlib import ExitStack

import numpy as np
import ml_dtypes

import concourse.bass as bass
import concourse.mybir as mybir
from concourse.bass_utils import run_bass_kernel_spmd

F32 = mybir.dt.float32
BF16 = mybir.dt.bfloat16
AF = mybir.ActivationFunctionType
ALU = mybir.AluOpType
AX = mybir.AxisListType

NCORES = 8
T = 2048
D = 2048
NT = T // 128
ND = D // 128
NH = 16
DH = 128
E = 8
R = 16
CAP = 5120
I1 = 1408
NJ1 = I1 // 128
J1 = 5632
NJS = J1 // 128
EPS = 1e-5

GC_W8 = 0
GC_LZ = 128
GC_LZ2 = 136
GC_EQ1 = 144
GC_EQ2 = 272
GC_CW = 400
GC_RB = 432
GC_G88 = 440
GC_MPREV = 448
GC_MSK = 456


def _consts():
    pos = np.arange(T, dtype=np.float64)
    dims = np.arange(DH // 2, dtype=np.float64)
    ang = np.power(10000.0, -dims / (DH // 2))
    rad = pos[None, :] * ang[:, None]
    cos = np.cos(rad).astype(np.float32)
    sin = np.sin(rad).astype(np.float32)
    cosF = np.concatenate([cos, cos], axis=0)
    sinF = np.concatenate([sin, -sin], axis=0)
    masks = []
    for r in (0, -128, -256, -384):
        dj = np.arange(128)[:, None]
        di = np.arange(512)[None, :]
        masks.append(((r + di - dj) >= 0).astype(np.float32))
    masks = np.stack(masks, 0).reshape(4 * 128, 512)
    e16 = np.zeros((8, 128), np.float32)
    for e in range(8):
        e16[e, e * 16:(e + 1) * 16] = 1.0
    return {
        "c_ident32": np.eye(128, dtype=np.float32),
        "c_identb": np.eye(128, dtype=np.float32).astype(ml_dtypes.bfloat16),
        "c_cosF": cosF.astype(ml_dtypes.bfloat16),
        "c_sinF": sinF.astype(ml_dtypes.bfloat16),
        "c_masks": masks.astype(ml_dtypes.bfloat16),
        "c_e16": e16.astype(ml_dtypes.bfloat16),
        "c_onesrow": np.ones((1, 128), np.float32).astype(ml_dtypes.bfloat16),
        "c_onescol": np.ones((128, 1), np.float32).astype(ml_dtypes.bfloat16),
        "c_ones32": np.ones((1, 128), np.float32),
    }


class P:
    def __init__(self, layers, fused):
        self.layers = layers
        self.fused = fused
        self.nc = bass.Bass("TRN2", target_bir_lowering=False, num_devices=NCORES)
        self.ctx = ExitStack()
        nc = self.nc
        self.sem = {}
        self.cnt = {}
        for name in ("W", "X", "P", "V", "A", "C"):
            self.sem[name] = self.ctx.enter_context(nc.semaphore(name))
            self.cnt[name] = 0
        self.lastwait = {}
        self.d = {}
        self.s = {}
        self._declare_dram()
        self._alloc_perm()

    def inc(self, ins, s, n=1):
        ins.then_inc(self.sem[s], n)
        self.cnt[s] += n
        return (s, self.cnt[s])

    def ev(self, s):
        return (s, self.cnt[s])

    def wait(self, eng, *evs):
        for e in evs:
            if e is None:
                continue
            s, v = e
            if v <= 0:
                continue
            key = (id(eng), s)
            if self.lastwait.get(key, -1) >= v:
                continue
            eng.wait_ge(self.sem[s], v)
            self.lastwait[key] = v

    def dma(self, eng, out, in_, s, *waits):
        if eng is self.nc.sync:
            eng = self.nc.gpsimd
        self.wait(eng, *waits)
        return self.inc(eng.dma_start(out, in_), s, 16)

    def dma_t(self, out, in_, *waits):
        self.wait(self.nc.sync, *waits)
        return self.inc(self.nc.sync.dma_start(out, in_, transpose=True), "X", 16)

    def V(self, ins):
        e = self.inc(ins, "V")
        self.wait(self.nc.vector, e)
        return e

    def A(self, ins):
        e = self.inc(ins, "A")
        self.wait(self.nc.scalar, e)
        return e

    def _declare_dram(self):
        nc, d, layers = self.nc, self.d, self.layers

        def din(name, shape, dt=F32):
            d[name] = nc.dram_tensor(name, list(shape), dt, kind="ExternalInput").ap()

        def dint(name, shape, dt=F32):
            d[name] = nc.dram_tensor(name, list(shape), dt, kind="Internal").ap()

        din("x", (T, D))
        d["xo"] = nc.dram_tensor("xo", [T, D], F32, kind="ExternalOutput").ap()
        din("c_ident32", (128, 128))
        din("c_identb", (128, 128), BF16)
        din("c_onesrow", (1, 128), BF16)
        din("c_onescol", (128, 1), BF16)
        din("c_ones32", (1, 128))
        dint("ybuf", (T, D))
        if 0 in layers:
            din("c_cosF", (128, T), BF16)
            din("c_sinF", (128, T), BF16)
            din("c_masks", (4 * 128, 512), BF16)
            din("L0_attn_w_aq", (NH, D, DH))
            din("L0_attn_w_ak", (NH, D, DH))
            din("L0_attn_w_av", (NH, D, DH))
            din("L0_attn_w_ao", (NH, DH, D))
        if 1 in layers:
            din("c_e16", (8, 128), BF16)
            din("L0_ffn_base_w", (D, D))
            din("L0_ffn_base_b", (D,))
            din("L0_ffn_router_w", (E, D))
            din("L0_ffn_router_b", (E,))
            din("L0_ffn_lora_A", (E, D, R))
            din("L0_ffn_lora_B", (E, R, D))
            dint("wscr0", (D, D), BF16)
        if 2 in layers:
            din("L1_attn_Wa", (64, D))
            din("L1_attn_ba", (64,))
            din("L1_attn_Wb", (64, D))
            din("L1_attn_bb", (64,))
            din("L1_attn_Wout", (D, 64))
            din("L1_attn_bout", (D,))
        if 3 in layers:
            din("c_maskprev", (8, 8))
            din("L1_ffn_router", (D, E))
            din("L1_ffn_w_up_gate", (E, D, 2 * I1))
            din("L1_ffn_w_down", (E, I1, D))
            din("L1_ffn_shared_w_up_gate", (D, 2 * J1))
            din("L1_ffn_shared_w_down", (J1, D))
            dint("hbuf", ((E + 4) * I1, T), BF16)
            dint("cc_in", (8, 1))
            dint("cc_out", (64, 1))
        if self.fused:
            for i in (1, 2, 3):
                dint(f"xres{i}", (T, D))

    def _alloc_perm(self):
        nc, s, ctx = self.nc, self.s, self.ctx

        def al(name, shape, dt):
            s[name] = ctx.enter_context(nc.sbuf_tensor(name, list(shape), dt))

        al("ident32", (128, 128), F32)
        al("identb", (128, 128), BF16)
        al("onesrow", (1, 128), BF16)
        al("onescol", (128, 1), BF16)
        al("ones32", (1, 128), F32)
        al("stats", (128, 16), F32)
        al("gates", (128, 464), F32)
        al("rwT", (128, 16 * 8), F32)
        al("brow", (1, 2048), BF16)
        al("srow", (1, 512), F32)
        al("scr32", (128, 6144), F32)
        al("constb", (128, 6144), BF16)
        s["pt"] = ctx.enter_context(nc.psum_tensor("pt", [128, 2048], F32))
        s["pb4"] = ctx.enter_context(nc.psum_tensor("pb4", [128, 512], F32))
        s["pb5"] = ctx.enter_context(nc.psum_tensor("pb5", [128, 512], F32))
        s["ptf"] = ctx.enter_context(nc.psum_tensor("ptf", [128, 512], F32))
        s["pb7"] = ctx.enter_context(nc.psum_tensor("pb7", [128, 512], F32))
        self.xSB = s["scr32"][:, 0:2048]
        self.ySB = s["scr32"][:, 2048:4096]
        self.xnSB = s["scr32"][:, 4096:6144]

    def load_consts(self):
        nc, s, d = self.nc, self.s, self.d
        for name in ("W", "X", "P", "V", "A", "C"):
            nc.sync.sem_clear(self.sem[name])
        nc.all_engine_barrier()
        evs = [self.dma(nc.sync, s["ident32"][:, :], d["c_ident32"], "X"),
               self.dma(nc.sync, s["identb"][:, :], d["c_identb"], "X"),
               self.dma(nc.sync, s["onesrow"][:, :], d["c_onesrow"], "X"),
               self.dma(nc.sync, s["onescol"][:, :], d["c_onescol"], "X"),
               self.dma(nc.sync, s["ones32"][:, :], d["c_ones32"], "X")]
        if 0 in self.layers:
            evs.append(self.dma(nc.sync, s["constb"][:, 0:2048], d["c_cosF"], "X"))
            evs.append(self.dma(nc.sync, s["constb"][:, 2048:4096], d["c_sinF"], "X"))
            for m in range(4):
                evs.append(self.dma(nc.sync,
                                    s["constb"][:, 4096 + m * 512:4096 + (m + 1) * 512],
                                    d["c_masks"][m * 128:(m + 1) * 128, :], "X"))
        for eng in (nc.tensor, nc.vector, nc.scalar):
            self.wait(eng, *evs)

    def bcast_col(self, src11, dst_col):
        nc, s = self.nc, self.s
        self.wait(nc.tensor, self.ev("V"))
        pe = self.inc(nc.tensor.matmul(s["pb4"][:, 0:1], s["ones32"][:, :], src11,
                                       start=True, stop=True), "P")
        self.wait(nc.scalar, pe)
        ae = self.A(nc.scalar.copy(dst_col, s["pb4"][:, 0:1]))
        self.wait(nc.tensor, ae)
        return ae

    def ln_tile(self, src, out):
        nc, s = self.nc, self.s
        vec, act = nc.vector, nc.scalar
        stats = s["stats"]
        nm, ss, rs = stats[:, 0:1], stats[:, 1:2], stats[:, 2:3]
        self.V(vec.reduce_sum(nm, src, axis=AX.X, negate=True))
        self.V(vec.tensor_scalar(nm, nm, 1.0 / D, None, ALU.mult))
        self.wait(act, self.ev("V"), self.ev("P"))
        acc = self.s["gates"][:, GC_LZ:GC_LZ + 4]
        for c in range(4):
            self.A(act.activation(s["pt"][:, c * 512:(c + 1) * 512],
                                    src[:, c * 512:(c + 1) * 512], AF.Square,
                                    bias=nm, scale=1.0,
                                    accum_out=acc[:, c:c + 1]))
        self.wait(vec, self.ev("A"))
        self.V(vec.tensor_add(acc[:, 0:1], acc[:, 0:1], acc[:, 1:2]))
        self.V(vec.tensor_add(acc[:, 2:3], acc[:, 2:3], acc[:, 3:4]))
        self.V(vec.tensor_add(ss, acc[:, 0:1], acc[:, 2:3]))
        self.V(vec.tensor_scalar(rs, ss, 1.0 / D, EPS, ALU.mult, ALU.add))
        self.wait(act, self.ev("V"))
        self.A(act.activation(rs, rs, AF.Ln))
        self.A(act.activation(rs, rs, AF.Exp, scale=-0.5))
        self.wait(vec, self.ev("A"))
        self.V(vec.tensor_scalar(out, src, nm, rs, ALU.add, ALU.mult))
        return self.ev("V")

    def top2_tile(self, L, tt, scale):
        nc, s = self.nc, self.s
        vec, act = nc.vector, nc.scalar
        g = s["gates"]
        eq1 = g[:, GC_EQ1 + tt * 8:GC_EQ1 + tt * 8 + 8]
        eq2 = g[:, GC_EQ2 + tt * 8:GC_EQ2 + tt * 8 + 8]
        cw = g[:, GC_CW + tt * 2:GC_CW + tt * 2 + 2]
        m8 = s["stats"][:, 8:16]
        l12 = s["stats"][:, 4:6]
        Lz = g[:, GC_LZ:GC_LZ + 8]
        Lz2 = g[:, GC_LZ2:GC_LZ2 + 8]
        self.V(vec.max(m8, L))
        self.V(vec.tensor_copy(l12, m8[:, 0:2]))
        self.V(vec.memset(m8[:, 1:8], -1e30))
        self.V(vec.match_replace(Lz, m8, L, -1e30))
        self.V(vec.tensor_tensor(eq1, L, Lz, ALU.not_equal))
        self.V(vec.tensor_copy(m8[:, 0:1], l12[:, 1:2]))
        self.V(vec.match_replace(Lz2, m8, Lz, -1e30))
        self.V(vec.tensor_tensor(eq2, Lz, Lz2, ALU.not_equal))
        self.V(vec.tensor_tensor(cw[:, 0:1], l12[:, 1:2], l12[:, 0:1], ALU.subtract))
        self.wait(act, self.ev("V"))
        self.A(act.activation(cw[:, 0:1], cw[:, 0:1], AF.Exp))
        self.wait(vec, self.ev("A"))
        self.V(vec.tensor_scalar(cw[:, 0:1], cw[:, 0:1], 1.0, None, ALU.add))
        self.V(vec.reciprocal(cw[:, 0:1], cw[:, 0:1]))
        self.V(vec.tensor_scalar(cw[:, 1:2], cw[:, 0:1], -scale, scale,
                                   ALU.mult, ALU.add))
        if scale != 1.0:
            self.V(vec.tensor_scalar(cw[:, 0:1], cw[:, 0:1], scale, None, ALU.mult))

    def router_prep(self, layer):
        nc, s, d = self.nc, self.s, self.d
        rwT = s["rwT"]
        if layer == 1:
            evs = []
            for j in range(ND):
                evs.append(self.dma(nc.sync, rwT[:, j * 8:(j + 1) * 8],
                                    d["L1_ffn_router"][j * 128:(j + 1) * 128, :], "X"))
            self.wait(nc.tensor, *evs)
            return
        tmp = s["scr32"][0:8, 0:2048]
        e = self.dma(nc.gpsimd, tmp, d["L0_ffn_router_w"], "W")
        self.wait(nc.tensor, e)
        for j in range(ND):
            pe = self.inc(nc.tensor.transpose(s["ptf"][:, 0:8],
                                              tmp[:, j * 128:(j + 1) * 128],
                                              s["ident32"][0:8, 0:8]), "P")
            self.wait(nc.scalar, pe)
            ae = self.A(nc.scalar.copy(rwT[:, j * 8:(j + 1) * 8], s["ptf"][:, 0:8]))
            self.wait(nc.tensor, ae)
        eb = self.dma(nc.sync, s["srow"][0:1, 0:8],
                      d["L0_ffn_router_b"].rearrange("(a e) -> a e", a=1), "X")
        self.wait(nc.tensor, eb)
        pe = self.inc(nc.tensor.matmul(s["pb4"][:, 0:8], s["ones32"][:, :], s["srow"][0:1, 0:8],
                                       start=True, stop=True), "P")
        self.wait(nc.scalar, pe)
        ae = self.A(nc.scalar.copy(s["gates"][:, GC_RB:GC_RB + 8], s["pb4"][:, 0:8]))
        self.wait(nc.tensor, ae)

    def pre_ln(self, x_src, router=None, xnRow=None):
        nc, s = self.nc, self.s
        sync, ten, vec, act = nc.sync, nc.tensor, nc.vector, nc.scalar
        xnT = s["xnT"]
        if router is not None:
            self.router_prep(router)
        xnt32 = self.ySB
        for tt in range(NT):
            ex = self.dma(sync, self.xSB, x_src[tt * 128:(tt + 1) * 128, :], "X",
                          self.ev("V"), self.ev("A"), self.ev("P"),
                          self.ev("X") if tt == 0 else None)
            self.wait(vec, ex)
            self.wait(act, ex)
            self.ln_tile(self.xSB, self.xnSB)
            if xnRow is not None:
                self.V(vec.tensor_copy(xnRow[:, tt * 2048:(tt + 1) * 2048], self.xnSB))
            self.wait(ten, self.ev("V"))
            for j in range(ND):
                bank = s["pt"][:, (j % 4) * 512:(j % 4) * 512 + 128]
                if j % 4 == 0:
                    self.wait(ten, self.ev("A"))
                pe = self.inc(ten.transpose(bank, self.xnSB[:, j * 128:(j + 1) * 128],
                                            s["ident32"][:, :]), "P")
                self.wait(act, pe)
                self.A(act.copy(
                    xnT[:, j * 2048 + tt * 128: j * 2048 + tt * 128 + 128], bank))
                if router is not None:
                    self.A(act.copy(xnt32[:, j * 128:(j + 1) * 128], bank))
            if router is not None:
                self.wait(ten, self.ev("A"))
                for j in range(ND):
                    pe = ten.matmul(s["pb4"][:, 0:8], xnt32[:, j * 128:(j + 1) * 128],
                                    s["rwT"][:, j * 8:(j + 1) * 8],
                                    start=(j == 0), stop=(j == ND - 1))
                pe = self.inc(pe, "P")
                self.wait(vec, pe)
                L = s["gates"][:, 0:8] if False else s["stats"][:, 8:16][:, 0:8]
                L = s["gates"][:, GC_MSK:GC_MSK + 8]
                if router == 1:
                    self.V(vec.tensor_copy(L, s["pb4"][:, 0:8]))
                else:
                    self.V(vec.tensor_tensor(L, s["pb4"][:, 0:8],
                                               s["gates"][:, GC_RB:GC_RB + 8], ALU.add))
                self.wait(ten, self.ev("V"))
                self.top2_tile(L, tt, 2.0 if router == 0 else 1.0)
                g = s["gates"]
                eq1 = g[:, GC_EQ1 + tt * 8:GC_EQ1 + tt * 8 + 8]
                eq2 = g[:, GC_EQ2 + tt * 8:GC_EQ2 + tt * 8 + 8]
                cw = g[:, GC_CW + tt * 2:GC_CW + tt * 2 + 2]
                if router == 0:
                    W8 = g[:, tt * 8:(tt + 1) * 8]
                    self.V(vec.tensor_scalar(W8, eq1, cw[:, 0:1], None, ALU.mult))
                    self.V(vec.scalar_tensor_tensor(W8, eq2, cw[:, 1:2], W8,
                                                      ALU.mult, ALU.add))
                    self.wait(ten, self.ev("V"))
                    pe = self.inc(ten.transpose(s["ptf"][0:8, 0:128], W8, s["ident32"][:, :]), "P")
                    self.wait(act, pe)
                    ae = self.A(act.copy(s["g0T"][0:8, tt * 128:(tt + 1) * 128],
                                           s["ptf"][0:8, 0:128]))
                    self.wait(ten, ae)
                else:
                    for eq, rname in ((eq1, "r_oh1"), (eq2, "r_oh2")):
                        self.wait(ten, self.ev("V"))
                        pe = self.inc(ten.transpose(s["ptf"][0:8, 0:128], eq,
                                                    s["ident32"][:, :]), "P")
                        self.wait(vec, pe)
                        ve = self.V(vec.tensor_copy(
                            s[rname][0:8, tt * 128:(tt + 1) * 128],
                            s["ptf"][0:8, 0:128]))
                        self.wait(ten, ve)

    def post_ln(self, x_src, x_dst, y_dram=None, y_tiles=None, tts=None):
        nc = self.nc
        sync, vec, act = nc.sync, nc.vector, nc.scalar
        for tt in (tts if tts is not None else range(NT)):
            if y_dram is not None:
                ey = self.dma(sync, self.ySB, y_dram[tt * 128:(tt + 1) * 128, :], "X",
                              self.ev("V"), self.ev("A"), self.ev("X"))
                ysrc = self.ySB
                self.wait(vec, ey)
                self.wait(act, ey)
            else:
                ysrc = y_tiles(tt)
            ex = self.dma(sync, self.xSB, x_src[tt * 128:(tt + 1) * 128, :], "X",
                          self.ev("V"), self.ev("X") if y_dram is None else None)
            self.ln_tile(ysrc, self.xnSB)
            self.wait(vec, ex)
            self.V(vec.tensor_add(self.xSB, self.xSB, self.xnSB))
            self.dma(sync, x_dst[tt * 128:(tt + 1) * 128, :], self.xSB, "X", self.ev("V"))

    def l0_attn(self):
        nc, s, d = self.nc, self.s, self.d
        sync, gps, ten, vec, act = nc.sync, nc.gpsimd, nc.tensor, nc.vector, nc.scalar
        xnT, o_all, scrb = s["xnT"], s["bbuf"], s["scrb"]
        cosF = s["constb"][:, 0:2048]
        sinF = s["constb"][:, 2048:4096]
        ms = [s["constb"][:, 4096 + m * 512:4096 + (m + 1) * 512] for m in range(4)]
        qT = scrb[:, 0:2048]
        kT = scrb[:, 2048:4096]
        vS = scrb[:, 4096:6144]
        expS = [scrb[:, 6144:6656], scrb[:, 6656:7168]]
        ws = [scrb[:, 7168:9216], scrb[:, 9216:11264]]
        pt, pb4, pb5, ptf = s["pt"], s["pb4"], s["pb5"], s["ptf"]
        stats = s["stats"]
        srow = s["srow"]
        wsrc = [d["L0_attn_w_aq"], d["L0_attn_w_ak"], d["L0_attn_w_av"]]
        slot_free = [None, None]
        load_ev = {}

        def load_w(h, which):
            slot = (3 * h + which) % 2
            src = wsrc[which][h].rearrange("(j p) d -> p j d", p=128)
            dst = ws[slot].rearrange("p (j d) -> p j d", j=ND)
            load_ev[(h, which)] = (slot, self.dma(gps, dst, src, "W",
                                                  slot_free[slot]))

        def proj_rope(h, which, dstT, amax_col):
            slot, wev = load_ev[(h, which)]
            self.wait(ten, wev, self.ev("A"))
            for j in range(ND):
                for c in range(4):
                    pe = ten.matmul(pt[:, c * 512:(c + 1) * 512],
                                    ws[slot][:, j * 128:(j + 1) * 128],
                                    xnT[:, j * 2048 + c * 512: j * 2048 + (c + 1) * 512],
                                    start=(j == 0), stop=(j == ND - 1))
            pev = self.inc(pe, "P")
            slot_free[slot] = pev
            self.wait(act, pev, self.ev("V"))
            for c in range(4):
                self.A(act.copy(vS[:, c * 512:(c + 1) * 512],
                                  pt[:, c * 512:(c + 1) * 512]))
            self.wait(vec, self.ev("A"))
            for c in range(4):
                t1, t2 = expS[0], expS[1]
                cc = slice(c * 512, (c + 1) * 512)
                self.V(vec.tensor_tensor(t1, vS[:, cc], cosF[:, cc], ALU.mult))
                self.V(vec.tensor_tensor(t2[0:64, :], vS[64:128, cc], sinF[64:128, cc],
                                           ALU.mult))
                self.V(vec.tensor_tensor(t2[64:128, :], vS[0:64, cc], sinF[0:64, cc],
                                           ALU.mult))
                self.V(vec.tensor_add(dstT[:, cc], t1, t2))
            self.V(vec.reduce_max(amax_col, dstT, axis=AX.X,
                                    apply_absolute_value=True))
            return self.ev("V")

        load_w(0, 0)
        load_w(0, 1)
        last_ydma = None
        for h in range(NH):
            evq = proj_rope(h, 0, qT, stats[:, 3:4])
            load_w(h, 2)
            evk = proj_rope(h, 1, kT, stats[:, 4:5])
            self.wait(ten, evq, evk)
            for i in range(2):
                pe = self.inc(ten.transpose(ptf[0:1, 0:128], stats[:, 3 + i:4 + i],
                                            s["ident32"][:, :]), "P")
                self.wait(vec, pe)
                ve = self.V(vec.reduce_max(srow[0:1, i:i + 1], ptf[0:1, 0:128],
                                             axis=AX.X))
                self.wait(ten, ve)
            self.V(vec.tensor_tensor(srow[0:1, 2:3], srow[0:1, 0:1], srow[0:1, 1:2],
                                       ALU.mult))
            self.V(vec.tensor_scalar(srow[0:1, 2:3], srow[0:1, 2:3], -1.0, None,
                                       ALU.mult))
            self.bcast_col(srow[0:1, 2:3], stats[:, 7:8])
            negG = stats[:, 7:8]
            slot, wev = load_ev[(h, 2)]
            self.wait(ten, wev)
            for tt in range(NT):
                bank = pt[:, (tt // 4) * 512 + (tt % 4) * 128:
                          (tt // 4) * 512 + (tt % 4) * 128 + 128]
                for j in range(ND):
                    pe = ten.matmul(bank,
                                    xnT[:, j * 2048 + tt * 128: j * 2048 + tt * 128 + 128],
                                    ws[slot][:, j * 128:(j + 1) * 128],
                                    start=(j == 0), stop=(j == ND - 1))
            pev = self.inc(pe, "P")
            slot_free[slot] = pev
            self.wait(act, pev, self.ev("V"))
            for c in range(4):
                self.A(act.copy(vS[:, c * 512:(c + 1) * 512],
                                  pt[:, c * 512:(c + 1) * 512]))
            self.wait(ten, self.ev("A"))
            if h + 1 < NH:
                load_w(h + 1, 0)
                load_w(h + 1, 1)
            scale_ve = None
            for ic in range(4):
                njt = min(NT, (ic + 1) * 4)
                expev = [scale_ve, None]
                sev = [None, None]
                exprd = [None, None]

                def s_mm(jt):
                    sbank = (pb4, pb5)[jt % 2]
                    self.wait(ten, exprd[jt % 2])
                    sev[jt % 2] = self.inc(
                        ten.matmul(sbank[:, :], kT[:, jt * 128:(jt + 1) * 128],
                                   qT[:, ic * 512:(ic + 1) * 512],
                                   start=True, stop=True), "P")

                s_mm(0)
                for jt in range(njt):
                    sbank = (pb4, pb5)[jt % 2]
                    self.wait(act, sev[jt % 2], expev[jt % 2])
                    ae = self.A(act.activation(expS[jt % 2], sbank[:, :], AF.Exp,
                                               bias=negG, scale=1.0 / DH))
                    exprd[jt % 2] = ae
                    ready = ae
                    off = ic * 512 - jt * 128
                    if -384 <= off <= 0:
                        mi = (-off) // 128
                        self.wait(vec, ae)
                        ready = self.V(vec.tensor_tensor(expS[jt % 2], expS[jt % 2],
                                                         ms[mi], ALU.mult))
                    if jt + 1 < njt:
                        s_mm(jt + 1)
                    self.wait(ten, ready)
                    pe = ten.matmul(pt[:, 0:512], vS[:, jt * 128:(jt + 1) * 128],
                                    expS[jt % 2], start=(jt == 0), stop=(jt == njt - 1))
                    pe = ten.matmul(pt[0:1, 512:1024], s["onescol"][:, :], expS[jt % 2],
                                    start=(jt == 0), stop=(jt == njt - 1))
                    expev[jt % 2] = self.inc(pe, "P")
                pev = self.ev("P")
                self.wait(vec, pev)
                ve = self.V(vec.reciprocal(s["srow"][0:1, 0:512], pt[0:1, 512:1024]))
                self.wait(ten, ve)
                pe = self.inc(ten.matmul(pt[:, 1024:1536], s["ones32"][:, :],
                                         s["srow"][0:1, 0:512], start=True, stop=True), "P")
                self.wait(act, pe)
                ae = self.A(act.copy(expS[0], pt[:, 1024:1536]))
                self.wait(vec, ae)
                ve = self.V(vec.tensor_tensor(
                    o_all[:, h * 2048 + ic * 512: h * 2048 + (ic + 1) * 512],
                    pt[:, 0:512], expS[0], ALU.mult))
                scale_ve = ve
                self.wait(ten, ve)
        wstage = scrb[:, 0:8192]
        ystage = self.ySB
        for oc in range(4):
            evs = []
            for h in range(NH):
                evs.append(self.dma(gps, wstage[:, h * 512:(h + 1) * 512],
                                    d["L0_attn_w_ao"][h][:, oc * 512:(oc + 1) * 512],
                                    "W", self.ev("P") if (oc == 0 and h == 0) else None))
            self.wait(ten, *evs)
            for tt in range(NT):
                bank = (pb4, pb5)[tt % 2]
                self.wait(ten, self.ev("A"))
                for hh in range(NH):
                    pe = ten.matmul(bank[:, :],
                                    o_all[:, hh * 2048 + tt * 128: hh * 2048 + tt * 128 + 128],
                                    wstage[:, hh * 512:(hh + 1) * 512],
                                    start=(hh == 0), stop=(hh == NH - 1))
                pe = self.inc(pe, "P")
                self.wait(act, pe, last_ydma)
                ae = self.A(act.copy(ystage[:, 0:512], bank[:, :]))
                last_ydma = self.dma(
                    sync, d["ybuf"][tt * 128:(tt + 1) * 128, oc * 512:(oc + 1) * 512],
                    ystage[:, 0:512], "X", ae)

    def l0_ffn(self):
        nc, s, d = self.nc, self.s, self.d
        sync, gps, ten, vec, act = nc.sync, nc.gpsimd, nc.tensor, nc.vector, nc.scalar
        xnT, baseWT, scrb = s["xnT"], s["bbuf"], s["scrb"]
        pt, pb4, pb5 = s["pt"], s["pb4"], s["pb5"]
        A_all = scrb[:, 0:2048]
        B_all = scrb[:, 2048:4096]
        gxs = scrb[:, 4096:6144]
        gmid = scrb[:, 6144:8192]
        g0Tb = scrb[:, 8192:10240]
        ec = self.dma(gps, d["wscr0"], d["L0_ffn_base_w"], "W", self.ev("P"))
        tev = []
        for j in range(ND):
            tev.append(self.dma_t(baseWT[:, j * 2048:(j + 1) * 2048],
                                  d["wscr0"][:, j * 128:(j + 1) * 128], ec))
        aev = []
        for e in range(E):
            src = d["L0_ffn_lora_A"][e].rearrange("(j p) r -> p j r", p=128)
            dst = A_all.rearrange("p (j er) -> p j er", j=ND)[:, :, e * R:(e + 1) * R]
            aev.append(self.dma(gps, dst, src, "W"))
        bev = self.dma(gps, B_all, d["L0_ffn_lora_B"].rearrange("e r d -> (e r) d"), "W")
        ebb = self.dma(gps, s["brow"][:, :], d["L0_ffn_base_b"].rearrange("(a d) -> a d", a=1), "W")
        ev = self.V(vec.tensor_copy(g0Tb[0:8, :], s["g0T"][0:8, :]))
        self.wait(ten, ev, bev, ebb, *aev)
        ystage = self.ySB
        for c in range(4):
            cc = slice(c * 512, (c + 1) * 512)
            pe = self.inc(ten.matmul(pb4[:, :], s["e16b"][:, :], g0Tb[0:8, cc],
                                     start=True, stop=True), "P")
            self.wait(act, pe)
            ae = self.A(act.copy(gxs[:, cc], pb4[:, :]))
            self.wait(ten, self.ev("A"))
            for j in range(ND):
                pe = ten.matmul(pb5[:, :], A_all[:, j * 128:(j + 1) * 128],
                                xnT[:, j * 2048 + c * 512: j * 2048 + (c + 1) * 512],
                                start=(j == 0), stop=(j == ND - 1))
            pe = self.inc(pe, "P")
            self.wait(vec, pe, ae)
            ve = self.V(vec.tensor_tensor(gmid[:, cc], pb5[:, :], gxs[:, cc],
                                            ALU.mult))
            self.wait(ten, ve)
        self.wait(ten, *tev)
        last_ydma = None
        evacb = [None, None, None, None]
        for tt in range(NT):
            for oc in range(4):
                bank = pt[:, oc * 512:(oc + 1) * 512]
                self.wait(ten, evacb[oc])
                for j in range(ND):
                    pe = ten.matmul(bank,
                                    xnT[:, j * 2048 + tt * 128: j * 2048 + tt * 128 + 128],
                                    baseWT[:, j * 2048 + oc * 512: j * 2048 + (oc + 1) * 512],
                                    start=(j == 0), stop=False)
                pe = ten.matmul(bank, s["onesrow"][:, :], s["brow"][0:1, oc * 512:(oc + 1) * 512],
                                start=False, stop=False)
                pe = ten.matmul(bank, gmid[:, tt * 128:(tt + 1) * 128],
                                B_all[:, oc * 512:(oc + 1) * 512], start=False, stop=True)
                pe = self.inc(pe, "P")
                self.wait(act, pe, last_ydma)
                evacb[oc] = self.A(act.copy(ystage[:, oc * 512:(oc + 1) * 512], bank))
            last_ydma = self.dma(sync, d["ybuf"][tt * 128:(tt + 1) * 128, :], ystage,
                                 "X", self.ev("A"))

    def l1_attn(self):
        nc, s, d = self.nc, self.s, self.d
        sync, gps, ten, vec, act = nc.sync, nc.gpsimd, nc.tensor, nc.vector, nc.scalar
        xnT, xnRow, scrb = s["xnT"], s["bbuf"], s["scrb"]
        pt, pb4 = s["pt"], s["pb4"]
        aT = s["scr32"][0:64, 0:2048]
        bT = s["scr32"][0:64, 2048:4096]
        hsT = s["scr32"][0:64, 4096:6144]
        WaTb = scrb[:, 0:1024]
        WbTb = scrb[:, 1024:2048]
        WoutTb = scrb[0:64, 2048:4096]
        hsb = scrb[0:64, 4096:6144]
        bcol = s["stats"][0:64, 8:10]
        wtmp = s["scr32"][0:64, 0:2048]
        for (wname, dstT) in (("L1_attn_Wa", WaTb), ("L1_attn_Wb", WbTb)):
            ew = self.dma(gps, wtmp, d[wname], "W", self.ev("P"), self.ev("V"),
                          self.ev("A"))
            self.wait(ten, ew)
            for j in range(ND):
                pe = self.inc(ten.transpose(s["ptf"][:, 0:64],
                                            wtmp[0:64, j * 128:(j + 1) * 128],
                                            s["ident32"][0:64, 0:64]), "P")
                self.wait(act, pe)
                ae = self.A(act.copy(dstT[:, j * 64:(j + 1) * 64], s["ptf"][:, 0:64]))
                self.wait(ten, ae)
        wotmp = s["scr32"][:, 2048:3072]
        ew = self.dma(gps, wotmp.rearrange("p (j c) -> p j c", j=ND),
                      d["L1_attn_Wout"].rearrange("(j p) c -> p j c", p=128), "W")
        self.wait(ten, ew)
        for j in range(ND):
            pe = self.inc(ten.transpose(s["ptf"][0:64, 0:128], wotmp[:, j * 64:(j + 1) * 64],
                                        s["ident32"][:, :]), "P")
            self.wait(act, pe)
            ae = self.A(act.copy(WoutTb[:, j * 128:(j + 1) * 128], s["ptf"][0:64, 0:128]))
            self.wait(ten, ae)
        eba = self.dma(sync, bcol[:, 0:1], d["L1_attn_ba"].rearrange("(p a) -> p a", a=1), "X")
        ebb = self.dma(sync, bcol[:, 1:2], d["L1_attn_bb"].rearrange("(p a) -> p a", a=1), "X")
        ebo = self.dma(gps, s["brow"][:, :], d["L1_attn_bout"].rearrange("(a c) -> a c", a=1), "W")
        self.wait(act, eba, ebb)
        for (wT, bias, dstT) in ((WaTb, bcol[:, 0:1], aT), (WbTb, bcol[:, 1:2], bT)):
            for c in range(4):
                for j in range(ND):
                    pe = ten.matmul(pb4[0:64, :], wT[:, j * 64:(j + 1) * 64],
                                    xnT[:, j * 2048 + c * 512: j * 2048 + (c + 1) * 512],
                                    start=(j == 0), stop=(j == ND - 1))
                pe = self.inc(pe, "P")
                self.wait(act, pe)
                ae = self.A(act.activation(dstT[:, c * 512:(c + 1) * 512], pb4[0:64, :],
                                             AF.Tanh, bias=bias, scale=1.0))
                self.wait(ten, ae)
        self.wait(vec, self.ev("A"))
        self.V(vec.tensor_tensor_scan(hsT, aT, bT, 0.0, ALU.mult, ALU.add))
        sv = self.V(vec.tensor_copy(hsb, hsT))
        self.wait(ten, sv, ebo)
        ystage = self.ySB
        last_ydma = None
        for tt in range(NT):
            self.wait(ten, self.ev("V"))
            for dc in range(4):
                bank = pt[:, dc * 512:(dc + 1) * 512]
                pe = ten.matmul(bank, hsb[:, tt * 128:(tt + 1) * 128],
                                WoutTb[:, dc * 512:(dc + 1) * 512], start=True, stop=False)
                pe = ten.matmul(bank, s["onesrow"][:, :], s["brow"][0:1, dc * 512:(dc + 1) * 512],
                                start=False, stop=True)
                self.inc(pe, "P")
            pev = self.ev("P")
            self.wait(vec, pev, last_ydma)
            ve = self.V(vec.tensor_add(ystage, pt[:, :],
                                         xnRow[:, tt * 2048:(tt + 1) * 2048]))
            last_ydma = self.dma(sync, d["ybuf"][tt * 128:(tt + 1) * 128, :], ystage,
                                 "X", ve)

    def l1_route_finish(self):
        nc, s, d = self.nc, self.s, self.d
        sync, gps, ten, vec, act = nc.sync, nc.gpsimd, nc.tensor, nc.vector, nc.scalar
        oh1 = s["r_oh1"][0:8, :]
        oh2 = s["r_oh2"][0:8, :]
        Ssum = s["r_S"][0:8, :]
        sel = s["r_sel"][0:8, :]
        incl = s["r_incl"][0:8, :]
        ones8 = s["r_ones"][0:8, :]
        g = s["gates"]
        self.V(vec.tensor_add(Ssum, oh1, oh2))
        self.V(vec.memset(ones8, 1.0))
        sv = self.V(vec.tensor_tensor_scan(incl, ones8, Ssum, 0.0, ALU.mult, ALU.add))
        cdma = self.dma(sync, d["cc_in"], incl[:, T - 1:T], "X", sv)
        self.wait(gps, cdma)
        cc = self.inc(gps.collective_compute(
            "AllGather", ALU.bypass, replica_groups=[list(range(NCORES))],
            ins=[d["cc_in"]], outs=[d["cc_out"]], unique_tensors="No"), "C", 1)
        g88 = g[0:8, GC_G88:GC_G88 + 8]
        mprev = g[0:8, GC_MPREV:GC_MPREV + 8]
        emp = self.dma(sync, mprev, d["c_maskprev"], "X")
        edg = self.dma(sync, g88, d["cc_out"].rearrange("(r e) a -> r (e a)", e=8), "X", cc)
        self.wait(ten, edg)
        pe = self.inc(ten.transpose(s["ptf"][0:8, 0:8], g88, s["ident32"][0:8, 0:8]), "P")
        self.wait(vec, pe, emp)
        self.V(vec.tensor_tensor(g[0:8, GC_MSK:GC_MSK + 8], s["ptf"][0:8, 0:8], mprev,
                                   ALU.mult))
        off = s["stats"][0:8, 8:9]
        ve = self.V(vec.reduce_sum(off, g[0:8, GC_MSK:GC_MSK + 8], axis=AX.X))
        self.wait(ten, ve)
        for k, oh in ((0, oh1), (1, oh2)):
            sub = Ssum if k == 0 else oh2
            self.V(vec.tensor_tensor(sel, incl, sub, ALU.subtract))
            self.V(vec.tensor_scalar(sel, sel, off, None, ALU.add))
            self.V(vec.tensor_tensor(sel, sel, oh, ALU.mult))
            for tt in range(NT):
                self.wait(ten, self.ev("V"))
                pe = self.inc(ten.transpose(s["ptf"][:, 0:8],
                                            sel[:, tt * 128:(tt + 1) * 128],
                                            s["ident32"][0:8, 0:8]), "P")
                self.wait(vec, pe)
                pos = s["stats"][:, 10:11]
                self.V(vec.reduce_sum(pos, s["ptf"][:, 0:8], axis=AX.X))
                val = s["stats"][:, 11:12]
                self.V(vec.tensor_scalar(val, pos, float(CAP), None, ALU.is_lt))
                cwk = g[:, GC_CW + tt * 2 + k: GC_CW + tt * 2 + k + 1]
                wk = s["stats"][:, 12 + k:13 + k]
                ve = self.V(vec.tensor_tensor(wk, cwk, val, ALU.mult))
                self.wait(ten, ve)
                eqbase = GC_EQ1 if k == 0 else GC_EQ2
                eq = g[:, eqbase + tt * 8:eqbase + tt * 8 + 8]
                W8 = g[:, tt * 8:(tt + 1) * 8]
                if k == 0:
                    self.V(vec.tensor_scalar(W8, eq, wk, None, ALU.mult))
                else:
                    self.V(vec.scalar_tensor_tensor(W8, eq, wk, W8, ALU.mult, ALU.add))

    def l1_ffn_a(self):
        nc, s, d = self.nc, self.s, self.d
        sync, gps, ten, vec, act = nc.sync, nc.gpsimd, nc.tensor, nc.vector, nc.scalar
        xnT = s["xnT"]
        pairs = [(s["pb4"], s["pb5"]), (s["ptf"], s["pb7"])]
        wst = [s["pascr"][:, 0:4096], s["pascr"][:, 4096:8192]]
        hrow = [s["pascr"][:, 8192:10240], s["pascr"][:, 10240:12288]]
        sg = [s["scr32"][:, 0:512], s["scr32"][:, 512:1024]]
        items = []
        for ei in range(E + 1):
            shared = (ei == E)
            nj = NJS if shared else NJ1
            for j in range(nj):
                if shared:
                    gsrc = d["L1_ffn_shared_w_up_gate"][:, j * 128:(j + 1) * 128]
                    usrc = d["L1_ffn_shared_w_up_gate"][:, J1 + j * 128: J1 + (j + 1) * 128]
                    basej = E * NJ1 * 128
                else:
                    gsrc = d["L1_ffn_w_up_gate"][ei][:, j * 128:(j + 1) * 128]
                    usrc = d["L1_ffn_w_up_gate"][ei][:, I1 + j * 128: I1 + (j + 1) * 128]
                    basej = ei * NJ1 * 128
                items.append((gsrc, usrc, basej + j * 128))
        wfree = [None, None]
        wev = {}

        def issue(k):
            slot = k % 2
            gsrc, usrc, _ = items[k]
            e1 = self.dma(gps, wst[slot][:, 0:2048].rearrange("p (k c) -> p k c", k=ND),
                          gsrc.rearrange("(k p) c -> p k c", p=128), "W", wfree[slot])
            e2 = self.dma(gps, wst[slot][:, 2048:4096].rearrange("p (k c) -> p k c", k=ND),
                          usrc.rearrange("(k p) c -> p k c", p=128), "W")
            wev[k] = (e1, e2)

        issue(0)
        issue(1)
        self.wait(self.nc.tensor, self.ev("V"), self.ev("A"))
        evac_done = [None, None]
        hdma = [None, None]
        for k, (gsrc, usrc, hbase) in enumerate(items):
            slot = k % 2
            hslot = k % 2
            self.wait(ten, *wev[k])
            pev = None
            for c in range(4):
                par = c % 2
                self.wait(ten, evac_done[par])
                for jj in range(ND):
                    ten.matmul(pairs[par][0][:, :], wst[slot][:, jj * 128:(jj + 1) * 128],
                               xnT[:, jj * 2048 + c * 512: jj * 2048 + (c + 1) * 512],
                               start=(jj == 0), stop=(jj == ND - 1))
                    pu = ten.matmul(pairs[par][1][:, :],
                                    wst[slot][:, 2048 + jj * 128: 2048 + (jj + 1) * 128],
                                    xnT[:, jj * 2048 + c * 512: jj * 2048 + (c + 1) * 512],
                                    start=(jj == 0), stop=(jj == ND - 1))
                pev = self.inc(pu, "P")
                self.wait(act, pev)
                ae = self.A(act.activation(sg[par], pairs[par][0][:, :], AF.Silu))
                self.wait(vec, ae, hdma[hslot] if c == 0 else None)
                ve = self.V(vec.tensor_tensor(hrow[hslot][:, c * 512:(c + 1) * 512],
                                              sg[par], pairs[par][1][:, :], ALU.mult))
                evac_done[par] = ve
            wfree[slot] = pev
            if k + 2 < len(items):
                issue(k + 2)
            hdma[hslot] = self.dma(sync, d["hbuf"][hbase: hbase + 128, :],
                                   hrow[hslot], "X", self.ev("V"))

    def l1_ffn_b(self):
        nc, s, d = self.nc, self.s, self.d
        sync, gps, ten, vec, act = nc.sync, nc.gpsimd, nc.tensor, nc.vector, nc.scalar
        pt = s["pt"]
        yacc = s["yacc"]
        wdB = [s["wdB"], s["wdB2"]]
        hst = [s["hst"][:, 0:1408], s["hst"][:, 1408:2816]]
        g = s["gates"]
        xsrc = d["xres3"] if self.fused else d["x"]
        self.hx_ev = self.ev("X")
        groups = [("r", e) for e in range(E)] + [("s", gi) for gi in range(4)]
        ve = self.V(vec.memset(yacc[:, :], 0.0))
        fold_ev = [None, None, None, None]
        wfree = [None, None]
        hslot_free = [None, None]
        wev = {}

        def wsrc_of(kind, gi):
            if kind == "r":
                return d["L1_ffn_w_down"][gi], gi * NJ1 * 128
            return d["L1_ffn_shared_w_down"][gi * I1:(gi + 1) * I1, :], (E + gi) * NJ1 * 128

        def issue_w(gk, kind, gi):
            slot = gk % 2
            wsrc, _ = wsrc_of(kind, gi)
            wev[gk] = self.dma(gps, wdB[slot][:, :].rearrange("p (j c) -> p j c", j=NJ1),
                               wsrc.rearrange("(j p) c -> p j c", p=128), "W", wfree[slot])

        for half in range(2):
            tts = list(range(half * 8, half * 8 + 8))
            self.wait(ten, self.ev("V"), self.ev("A"))
            for gidx, (kind, gi) in enumerate(groups):
                gk = half * 12 + gidx
                if gk == 0:
                    issue_w(0, kind, gi)
                if gk + 1 < 24:
                    nk, ngi = groups[(gk + 1) % 12]
                    issue_w(gk + 1, nk, ngi)
                slot = gk % 2
                _, basej = wsrc_of(kind, gi)

                def issue_h(ti):
                    tt = tts[ti]
                    return self.dma(sync, hst[ti % 2].rearrange("p (j c) -> p j c", j=NJ1),
                                    d["hbuf"][basej:basej + NJ1 * 128,
                                              tt * 128:(tt + 1) * 128].rearrange(
                                        "(j p) c -> p j c", p=128), "X",
                                    hslot_free[ti % 2], self.hx_ev)

                hready = [None] * 8
                hready[0] = issue_h(0)
                for ti, tt in enumerate(tts):
                    if ti + 1 < 8:
                        hready[ti + 1] = issue_h(ti + 1)
                    self.wait(ten, hready[ti], wev[gk])
                    for dc in range(4):
                        bank = pt[:, dc * 512:(dc + 1) * 512]
                        self.wait(ten, fold_ev[dc])
                        for j in range(NJ1):
                            pe = ten.matmul(bank, hst[ti % 2][:, j * 128:(j + 1) * 128],
                                            wdB[slot][:, j * 2048 + dc * 512: j * 2048 + (dc + 1) * 512],
                                            start=(j == 0), stop=(j == NJ1 - 1))
                        pe = self.inc(pe, "P")
                        self.wait(vec, pe)
                        ys = yacc[:, ti * 2048 + dc * 512: ti * 2048 + (dc + 1) * 512]
                        if kind == "r":
                            wcol = g[:, tt * 8 + gi: tt * 8 + gi + 1]
                            ve = self.V(vec.scalar_tensor_tensor(
                                ys, bank, wcol, ys, ALU.mult, ALU.add))
                        else:
                            ve = self.V(vec.tensor_add(ys, ys, bank))
                        fold_ev[dc] = ve
                    hslot_free[ti % 2] = self.ev("P")
                wfree[slot] = self.ev("P")
            self.post_ln(xsrc, d["xo"],
                         y_tiles=(lambda h: (lambda tt: yacc[:, (tt - h * 8) * 2048:
                                                             (tt - h * 8 + 1) * 2048]))(half),
                         tts=tts)
            if half == 0:
                self.wait(vec, self.ev("X"))
                self.V(vec.memset(yacc[:, :], 0.0))

    def emit(self):
        nc, s, d = self.nc, self.s, self.d
        self.load_consts()
        layers = self.layers
        fused = self.fused
        xsrc = d["x"]
        with ExitStack() as big:
            s["xnT"] = big.enter_context(nc.sbuf_tensor("xnT", [128, ND * 2048], BF16))
            if 0 in layers or 1 in layers or 2 in layers:
                with ExitStack() as mid:
                    s["bbuf"] = mid.enter_context(
                        nc.sbuf_tensor("bbuf", [128, ND * 2048], BF16))
                    s["scrb"] = mid.enter_context(
                        nc.sbuf_tensor("scrb", [128, 11264], BF16))
                    if 1 in layers:
                        s["g0T"] = mid.enter_context(nc.sbuf_tensor("g0T", [8, 2048], F32))
                        s["e16b"] = mid.enter_context(nc.sbuf_tensor("e16b", [8, 128], BF16))
                        self.wait(nc.tensor,
                                  self.dma(nc.sync, s["e16b"][:, :], d["c_e16"], "X"))
                    if 0 in layers:
                        self.pre_ln(xsrc)
                        self.l0_attn()
                        dst = d["xres1"] if fused else d["xo"]
                        self.post_ln(xsrc, dst, y_dram=d["ybuf"])
                        xsrc = dst
                    if 1 in layers:
                        self.pre_ln(xsrc, router=0)
                        self.l0_ffn()
                        dst = d["xres2"] if fused else d["xo"]
                        self.post_ln(xsrc, dst, y_dram=d["ybuf"])
                        xsrc = dst
                    if 2 in layers:
                        self.pre_ln(xsrc, xnRow=s["bbuf"])
                        self.l1_attn()
                        dst = d["xres3"] if fused else d["xo"]
                        self.post_ln(xsrc, dst, y_dram=d["ybuf"])
                        xsrc = dst
            if 3 in layers:
                with ExitStack() as mid:
                    for rn in ("r_oh1", "r_oh2", "r_S", "r_sel", "r_incl", "r_ones"):
                        s[rn] = mid.enter_context(
                            nc.sbuf_tensor(rn, [8, 2048], F32))
                    self.pre_ln(xsrc, router=1)
                    self.l1_route_finish()
                    with ExitStack() as inner:
                        s["pascr"] = inner.enter_context(
                            nc.sbuf_tensor("pascr", [128, 12288], BF16))
                        self.l1_ffn_a()
        if 3 in layers:
            with ExitStack() as mid:
                s["yacc"] = mid.enter_context(nc.sbuf_tensor("yacc", [128, 8 * 2048], F32))
                s["wdB"] = mid.enter_context(nc.sbuf_tensor("wdB", [128, NJ1 * 2048], BF16))
                s["wdB2"] = mid.enter_context(nc.sbuf_tensor("wdB2", [128, NJ1 * 2048], BF16))
                s["hst"] = mid.enter_context(nc.sbuf_tensor("hst", [128, 2 * 1408], BF16))
                self.l1_ffn_b()
        nc = self.nc
        nc.vector.wait_ge(self.sem["V"], self.cnt["V"])
        nc.scalar.wait_ge(self.sem["A"], self.cnt["A"])
        nc.tensor.wait_ge(self.sem["P"], self.cnt["P"])
        nc.gpsimd.wait_ge(self.sem["W"], self.cnt["W"])
        if self.cnt["C"]:
            nc.gpsimd.wait_ge(self.sem["C"], self.cnt["C"])
        nc.sync.wait_ge(self.sem["X"], self.cnt["X"])
        nc.all_engine_barrier()
        self.ctx.close()
        return self.nc


_CONSTS = _consts()


def _in_maps(layers, inputs, xs):
    maps = []
    need = {"c_ident32", "c_identb", "c_onesrow", "c_onescol", "c_ones32"}
    wnames = []
    if 0 in layers:
        need |= {"c_cosF", "c_sinF", "c_masks"}
        wnames += ["L0_attn_w_aq", "L0_attn_w_ak", "L0_attn_w_av", "L0_attn_w_ao"]
    if 1 in layers:
        need |= {"c_e16"}
        wnames += ["L0_ffn_base_w", "L0_ffn_base_b", "L0_ffn_router_w",
                   "L0_ffn_router_b", "L0_ffn_lora_A", "L0_ffn_lora_B"]
    if 2 in layers:
        wnames += ["L1_attn_Wa", "L1_attn_ba", "L1_attn_Wb", "L1_attn_bb",
                   "L1_attn_Wout", "L1_attn_bout"]
    if 3 in layers:
        wnames += ["L1_ffn_router", "L1_ffn_w_up_gate", "L1_ffn_w_down",
                   "L1_ffn_shared_w_up_gate", "L1_ffn_shared_w_down"]
    for c in range(NCORES):
        m = {"x": np.ascontiguousarray(xs[c])}
        for n in need:
            m[n] = _CONSTS[n]
        for n in wnames:
            m[n] = inputs[n]
        if 3 in layers:
            mp = np.zeros((8, 8), np.float32)
            mp[:, :c] = 1.0
            m["c_maskprev"] = mp
        maps.append(m)
    return maps


def kernel(**inputs):
    inputs = {k: np.asarray(v) for k, v in inputs.items()}
    x = inputs["L0_attn_x"]
    xs = [np.asarray(x[c]) for c in range(NCORES)]
    layers = [0, 1, 2, 3]
    nc = P(layers, fused=True).emit()
    maps = _in_maps(layers, inputs, xs)
    res = run_bass_kernel_spmd(nc, maps, core_ids=list(range(NCORES)))
    return np.stack([res.results[c]["xo"] for c in range(NCORES)], axis=0)


# version 12 (final)
# speedup vs baseline: 1.4019x; 1.0900x over previous
from contextlib import ExitStack

import numpy as np
import ml_dtypes

import concourse.bass as bass
import concourse.mybir as mybir
from concourse.bass_utils import run_bass_kernel_spmd

F32 = mybir.dt.float32
BF16 = mybir.dt.bfloat16
AF = mybir.ActivationFunctionType
ALU = mybir.AluOpType
AX = mybir.AxisListType

NCORES = 8
T = 2048
D = 2048
NT = T // 128
ND = D // 128
NH = 16
DH = 128
E = 8
R = 16
CAP = 5120
I1 = 1408
NJ1 = I1 // 128
J1 = 5632
NJS = J1 // 128
EPS = 1e-5

GC_W8 = 0
GC_LZ = 128
GC_LZ2 = 136
GC_EQ1 = 144
GC_EQ2 = 272
GC_CW = 400
GC_RB = 432
GC_G88 = 440
GC_MPREV = 128
GC_MSK = 136


def _consts():
    pos = np.arange(T, dtype=np.float64)
    dims = np.arange(DH // 2, dtype=np.float64)
    ang = np.power(10000.0, -dims / (DH // 2))
    rad = pos[None, :] * ang[:, None]
    cos = np.cos(rad).astype(np.float32)
    sin = np.sin(rad).astype(np.float32)
    cosF = np.concatenate([cos, cos], axis=0)
    sinF = np.concatenate([sin, -sin], axis=0)
    masks = []
    for r in (0, -128, -256, -384):
        dj = np.arange(128)[:, None]
        di = np.arange(512)[None, :]
        masks.append(((r + di - dj) >= 0).astype(np.float32))
    masks = np.stack(masks, 0).reshape(4 * 128, 512)
    e16 = np.zeros((8, 128), np.float32)
    for e in range(8):
        e16[e, e * 16:(e + 1) * 16] = 1.0
    return {
        "c_ident32": np.eye(128, dtype=np.float32),
        "c_identb": np.eye(128, dtype=np.float32).astype(ml_dtypes.bfloat16),
        "c_cosF": cosF.astype(ml_dtypes.bfloat16),
        "c_sinF": sinF.astype(ml_dtypes.bfloat16),
        "c_masks": masks.astype(ml_dtypes.bfloat16),
        "c_e16": e16.astype(ml_dtypes.bfloat16),
        "c_onesrow": np.ones((1, 128), np.float32).astype(ml_dtypes.bfloat16),
        "c_onescol": np.ones((128, 1), np.float32).astype(ml_dtypes.bfloat16),
        "c_ones32": np.ones((1, 128), np.float32),
    }


class P:
    def __init__(self, layers, fused):
        self.layers = layers
        self.fused = fused
        self.nc = bass.Bass("TRN2", target_bir_lowering=False, num_devices=NCORES)
        self.ctx = ExitStack()
        nc = self.nc
        self.sem = {}
        self.cnt = {}
        for name in ("W", "X", "P", "V", "A", "C"):
            self.sem[name] = self.ctx.enter_context(nc.semaphore(name))
            self.cnt[name] = 0
        self.lastwait = {}
        self.d = {}
        self.s = {}
        self._declare_dram()
        self._alloc_perm()

    def inc(self, ins, s, n=1):
        ins.then_inc(self.sem[s], n)
        self.cnt[s] += n
        return (s, self.cnt[s])

    def ev(self, s):
        return (s, self.cnt[s])

    def wait(self, eng, *evs):
        for e in evs:
            if e is None:
                continue
            s, v = e
            if v <= 0:
                continue
            key = (id(eng), s)
            if self.lastwait.get(key, -1) >= v:
                continue
            eng.wait_ge(self.sem[s], v)
            self.lastwait[key] = v

    def dma(self, eng, out, in_, s, *waits):
        if eng is self.nc.sync:
            eng = self.nc.gpsimd
        self.wait(eng, *waits)
        return self.inc(eng.dma_start(out, in_), s, 16)

    def dma_h(self, out, in_, s, *waits):
        self.wait(self.nc.gpsimd, *waits)
        return self.inc(self.nc.gpsimd.dma_start(out, in_), s, 16)

    def dma_t(self, out, in_, *waits):
        self.wait(self.nc.sync, *waits)
        return self.inc(self.nc.sync.dma_start(out, in_, transpose=True), "X", 16)

    def V(self, ins):
        e = self.inc(ins, "V")
        self.wait(self.nc.vector, e)
        return e

    def A(self, ins):
        e = self.inc(ins, "A")
        self.wait(self.nc.scalar, e)
        return e

    def _declare_dram(self):
        nc, d, layers = self.nc, self.d, self.layers

        def din(name, shape, dt=F32):
            d[name] = nc.dram_tensor(name, list(shape), dt, kind="ExternalInput").ap()

        def dint(name, shape, dt=F32):
            d[name] = nc.dram_tensor(name, list(shape), dt, kind="Internal").ap()

        din("x", (T, D))
        d["xo"] = nc.dram_tensor("xo", [T, D], F32, kind="ExternalOutput").ap()
        din("c_ident32", (128, 128))
        din("c_identb", (128, 128), BF16)
        din("c_onesrow", (1, 128), BF16)
        din("c_onescol", (128, 1), BF16)
        din("c_ones32", (1, 128))
        dint("ybuf", (T, D))
        if 0 in layers:
            din("c_cosF", (128, T), BF16)
            din("c_sinF", (128, T), BF16)
            din("c_masks", (4 * 128, 512), BF16)
            din("L0_attn_w_aq", (NH, D, DH))
            din("L0_attn_w_ak", (NH, D, DH))
            din("L0_attn_w_av", (NH, D, DH))
            din("L0_attn_w_ao", (NH, DH, D))
        if 1 in layers:
            din("c_e16", (8, 128), BF16)
            din("L0_ffn_base_w", (D, D))
            din("L0_ffn_base_b", (D,))
            din("L0_ffn_router_w", (E, D))
            din("L0_ffn_router_b", (E,))
            din("L0_ffn_lora_A", (E, D, R))
            din("L0_ffn_lora_B", (E, R, D))
            dint("wscr0", (D, D), BF16)
        if 2 in layers:
            din("L1_attn_Wa", (64, D))
            din("L1_attn_ba", (64,))
            din("L1_attn_Wb", (64, D))
            din("L1_attn_bb", (64,))
            din("L1_attn_Wout", (D, 64))
            din("L1_attn_bout", (D,))
        if 3 in layers:
            din("c_maskprev", (8, 8))
            din("L1_ffn_router", (D, E))
            din("L1_ffn_w_up_gate", (E, D, 2 * I1))
            din("L1_ffn_w_down", (E, I1, D))
            din("L1_ffn_shared_w_up_gate", (D, 2 * J1))
            din("L1_ffn_shared_w_down", (J1, D))
            dint("hbuf", ((E + 4) * I1, T), BF16)
            dint("cc_in", (8, 1))
            dint("cc_out", (64, 1))
        if self.fused:
            for i in (1, 2, 3):
                dint(f"xres{i}", (T, D))

    def _alloc_perm(self):
        nc, s, ctx = self.nc, self.s, self.ctx

        def al(name, shape, dt):
            s[name] = ctx.enter_context(nc.sbuf_tensor(name, list(shape), dt))

        al("ident32", (128, 128), F32)
        al("onesrow", (1, 128), BF16)
        al("onescol", (128, 1), BF16)
        al("ones32", (1, 128), F32)
        al("stats", (128, 16), F32)
        al("gates", (128, 456), F32)
        al("rwT", (128, 16 * 8), F32)
        al("brow", (1, 2048), BF16)
        al("srow", (1, 512), F32)
        al("scr32", (128, 8192), F32)
        s["pt"] = ctx.enter_context(nc.psum_tensor("pt", [128, 2048], F32))
        s["pb4"] = ctx.enter_context(nc.psum_tensor("pb4", [128, 512], F32))
        s["pb5"] = ctx.enter_context(nc.psum_tensor("pb5", [128, 512], F32))
        s["ptf"] = ctx.enter_context(nc.psum_tensor("ptf", [128, 512], F32))
        s["pb7"] = ctx.enter_context(nc.psum_tensor("pb7", [128, 512], F32))
        self.xSB = s["scr32"][:, 0:2048]
        self.ySB = s["scr32"][:, 2048:4096]
        self.xnSB = s["scr32"][:, 4096:6144]
        self.xslots = [s["scr32"][:, 0:2048], s["scr32"][:, 6144:8192]]
        self.rot3 = [s["scr32"][:, 0:2048], s["scr32"][:, 2048:4096],
                     s["scr32"][:, 6144:8192]]

    def load_consts(self):
        nc, s, d = self.nc, self.s, self.d
        for name in ("W", "X", "P", "V", "A", "C"):
            nc.sync.sem_clear(self.sem[name])
        nc.all_engine_barrier()
        evs = [self.dma(nc.sync, s["ident32"][:, :], d["c_ident32"], "X"),
               self.dma(nc.sync, s["onesrow"][:, :], d["c_onesrow"], "X"),
               self.dma(nc.sync, s["onescol"][:, :], d["c_onescol"], "X"),
               self.dma(nc.sync, s["ones32"][:, :], d["c_ones32"], "X")]
        ve = self.V(nc.vector.memset(s["gates"][:, 448:449], EPS))
        for eng in (nc.tensor, nc.vector, nc.scalar):
            self.wait(eng, *evs)
        self.wait(nc.scalar, ve)

    def bcast_col(self, src11, dst_col):
        nc, s = self.nc, self.s
        self.wait(nc.tensor, self.ev("V"))
        pe = self.inc(nc.tensor.matmul(s["pb4"][:, 0:1], s["ones32"][:, :], src11,
                                       start=True, stop=True), "P")
        self.wait(nc.scalar, pe)
        ae = self.A(nc.scalar.copy(dst_col, s["pb4"][:, 0:1]))
        self.wait(nc.tensor, ae)
        return ae

    def ln_tile(self, src, out):
        nc, s = self.nc, self.s
        vec, act = nc.vector, nc.scalar
        stats = s["stats"]
        nm, rs = stats[:, 0:1], stats[:, 2:3]
        self.V(vec.reduce_sum(nm, src, axis=AX.X, negate=True))
        self.V(vec.tensor_scalar(nm, nm, 1.0 / D, None, ALU.mult))
        self.wait(act, self.ev("V"), self.ev("P"))
        acc = self.s["gates"][:, GC_LZ:GC_LZ + 4]
        for c in range(4):
            self.A(act.activation(s["pt"][:, c * 512:(c + 1) * 512],
                                  src[:, c * 512:(c + 1) * 512], AF.Square,
                                  bias=nm, scale=1.0,
                                  accum_out=acc[:, c:c + 1]))
        self.wait(vec, self.ev("A"))
        self.V(vec.tensor_add(acc[:, 0:1], acc[:, 0:1], acc[:, 1:2]))
        self.V(vec.tensor_add(acc[:, 2:3], acc[:, 2:3], acc[:, 3:4]))
        self.V(vec.tensor_add(acc[:, 0:1], acc[:, 0:1], acc[:, 2:3]))
        self.wait(act, self.ev("V"))
        self.A(act.activation(rs, acc[:, 0:1], AF.Ln, bias=s["gates"][:, 448:449],
                              scale=1.0 / D))
        self.A(act.activation(rs, rs, AF.Exp, scale=-0.5))
        self.wait(vec, self.ev("A"))
        self.V(vec.tensor_scalar(out, src, nm, rs, ALU.add, ALU.mult))
        return self.ev("V")

    def top2_tile(self, L, tt, scale):
        nc, s = self.nc, self.s
        vec, act = nc.vector, nc.scalar
        g = s["gates"]
        eq1 = g[:, GC_EQ1 + tt * 8:GC_EQ1 + tt * 8 + 8]
        eq2 = g[:, GC_EQ2 + tt * 8:GC_EQ2 + tt * 8 + 8]
        cw = g[:, GC_CW + tt * 2:GC_CW + tt * 2 + 2]
        m8 = s["stats"][:, 8:16]
        l12 = s["stats"][:, 4:6]
        Lz = g[:, GC_LZ:GC_LZ + 8]
        Lz2 = g[:, GC_LZ2:GC_LZ2 + 8]
        self.V(vec.max(m8, L))
        self.V(vec.tensor_copy(l12, m8[:, 0:2]))
        self.V(vec.memset(m8[:, 1:8], -1e30))
        self.V(vec.match_replace(Lz, m8, L, -1e30))
        self.V(vec.tensor_tensor(eq1, L, Lz, ALU.not_equal))
        self.V(vec.tensor_copy(m8[:, 0:1], l12[:, 1:2]))
        self.V(vec.match_replace(Lz2, m8, Lz, -1e30))
        self.V(vec.tensor_tensor(eq2, Lz, Lz2, ALU.not_equal))
        self.V(vec.tensor_tensor(cw[:, 0:1], l12[:, 1:2], l12[:, 0:1], ALU.subtract))
        self.wait(act, self.ev("V"))
        self.A(act.activation(cw[:, 0:1], cw[:, 0:1], AF.Exp))
        self.wait(vec, self.ev("A"))
        self.V(vec.tensor_scalar(cw[:, 0:1], cw[:, 0:1], 1.0, None, ALU.add))
        self.V(vec.reciprocal(cw[:, 0:1], cw[:, 0:1]))
        self.V(vec.tensor_scalar(cw[:, 1:2], cw[:, 0:1], -scale, scale,
                                   ALU.mult, ALU.add))
        if scale != 1.0:
            self.V(vec.tensor_scalar(cw[:, 0:1], cw[:, 0:1], scale, None, ALU.mult))

    def router_prep(self, layer):
        nc, s, d = self.nc, self.s, self.d
        rwT = s["rwT"]
        if layer == 1:
            evs = []
            for j in range(ND):
                evs.append(self.dma(nc.sync, rwT[:, j * 8:(j + 1) * 8],
                                    d["L1_ffn_router"][j * 128:(j + 1) * 128, :], "X"))
            self.wait(nc.tensor, *evs)
            return
        tmp = s["scr32"][0:8, 0:2048]
        e = self.dma(nc.gpsimd, tmp, d["L0_ffn_router_w"], "W")
        self.wait(nc.tensor, e)
        for j in range(ND):
            pe = self.inc(nc.tensor.transpose(s["ptf"][:, 0:8],
                                              tmp[:, j * 128:(j + 1) * 128],
                                              s["ident32"][0:8, 0:8]), "P")
            self.wait(nc.scalar, pe)
            ae = self.A(nc.scalar.copy(rwT[:, j * 8:(j + 1) * 8], s["ptf"][:, 0:8]))
            self.wait(nc.tensor, ae)
        eb = self.dma(nc.sync, s["srow"][0:1, 0:8],
                      d["L0_ffn_router_b"].rearrange("(a e) -> a e", a=1), "X")
        self.wait(nc.tensor, eb)
        pe = self.inc(nc.tensor.matmul(s["pb4"][:, 0:8], s["ones32"][:, :], s["srow"][0:1, 0:8],
                                       start=True, stop=True), "P")
        self.wait(nc.scalar, pe)
        ae = self.A(nc.scalar.copy(s["gates"][:, GC_RB:GC_RB + 8], s["pb4"][:, 0:8]))
        self.wait(nc.tensor, ae)

    def pre_ln(self, x_src, router=None, xnRow=None):
        nc, s = self.nc, self.s
        sync, ten, vec, act = nc.sync, nc.tensor, nc.vector, nc.scalar
        xnT = s["xnT"]
        if router is not None:
            self.router_prep(router)
        xnt32 = self.ySB
        xev = [None, None]
        used = [self.ev("P"), self.ev("P")]

        def load_x(tt):
            xev[tt % 2] = self.dma_h(self.xslots[tt % 2],
                                     x_src[tt * 128:(tt + 1) * 128, :], "X",
                                     used[tt % 2],
                                     self.ev("X") if tt == 0 else None)

        load_x(0)
        for tt in range(NT):
            xSB = self.xslots[tt % 2]
            if tt + 1 < NT:
                load_x(tt + 1)
            ex = xev[tt % 2]
            self.wait(vec, ex)
            self.wait(act, ex)
            self.ln_tile(xSB, self.xnSB)
            used[tt % 2] = self.ev("V")
            if xnRow is not None:
                self.V(vec.tensor_copy(xnRow[:, tt * 2048:(tt + 1) * 2048], self.xnSB))
            used[tt % 2] = self.ev("V")
            self.wait(ten, self.ev("V"))
            for j4 in range(4):
                self.wait(ten, self.ev("A"))
                for jj in range(4):
                    j = j4 * 4 + jj
                    bank = s["pt"][:, jj * 512:jj * 512 + 128]
                    pe = self.inc(ten.transpose(bank, self.xnSB[:, j * 128:(j + 1) * 128],
                                                s["ident32"][:, :]), "P")
                self.wait(act, pe)
                psrc = s["pt"][:, :].rearrange("p (a c) -> p a c", a=4)[:, :, 0:128]
                pdst = xnT[:, tt * 128: tt * 128 + 128].rearrange(
                    "p c -> p () c")[:, 0:1, :] if False else None
                dst = xnT.rearrange("p (j c) -> p j c", j=ND)[:, j4 * 4:(j4 + 1) * 4,
                                                             tt * 128: tt * 128 + 128]
                self.A(act.copy(dst, psrc))
                if router is not None:
                    dst32 = xnt32.rearrange("p (j c) -> p j c", j=ND)[:, j4 * 4:(j4 + 1) * 4, :]
                    self.A(act.copy(dst32, psrc))
            if router is not None:
                self.wait(ten, self.ev("A"))
                for j in range(ND):
                    pe = ten.matmul(s["pb4"][:, 0:8], xnt32[:, j * 128:(j + 1) * 128],
                                    s["rwT"][:, j * 8:(j + 1) * 8],
                                    start=(j == 0), stop=(j == ND - 1))
                pe = self.inc(pe, "P")
                self.wait(vec, pe)
                L = s["gates"][:, 0:8] if False else s["stats"][:, 8:16][:, 0:8]
                L = s["gates"][:, GC_MSK:GC_MSK + 8]
                if router == 1:
                    self.V(vec.tensor_copy(L, s["pb4"][:, 0:8]))
                else:
                    self.V(vec.tensor_tensor(L, s["pb4"][:, 0:8],
                                               s["gates"][:, GC_RB:GC_RB + 8], ALU.add))
                self.wait(ten, self.ev("V"))
                self.top2_tile(L, tt, 2.0 if router == 0 else 1.0)
                g = s["gates"]
                eq1 = g[:, GC_EQ1 + tt * 8:GC_EQ1 + tt * 8 + 8]
                eq2 = g[:, GC_EQ2 + tt * 8:GC_EQ2 + tt * 8 + 8]
                cw = g[:, GC_CW + tt * 2:GC_CW + tt * 2 + 2]
                if router == 0:
                    W8 = g[:, tt * 8:(tt + 1) * 8]
                    self.V(vec.tensor_scalar(W8, eq1, cw[:, 0:1], None, ALU.mult))
                    self.V(vec.scalar_tensor_tensor(W8, eq2, cw[:, 1:2], W8,
                                                      ALU.mult, ALU.add))
                    self.wait(ten, self.ev("V"))
                    pe = self.inc(ten.transpose(s["ptf"][0:8, 0:128], W8, s["ident32"][:, :]), "P")
                    self.wait(act, pe)
                    ae = self.A(act.copy(s["g0T"][0:8, tt * 128:(tt + 1) * 128],
                                           s["ptf"][0:8, 0:128]))
                    self.wait(ten, ae)
                else:
                    for eq, rname in ((eq1, "r_oh1"), (eq2, "r_oh2")):
                        self.wait(ten, self.ev("V"))
                        pe = self.inc(ten.transpose(s["ptf"][0:8, 0:128], eq,
                                                    s["ident32"][:, :]), "P")
                        self.wait(vec, pe)
                        ve = self.V(vec.tensor_copy(
                            s[rname][0:8, tt * 128:(tt + 1) * 128],
                            s["ptf"][0:8, 0:128]))
                        self.wait(ten, ve)

    def post_ln(self, x_src, x_dst, y_dram=None, y_tiles=None, tts=None):
        nc = self.nc
        sync, vec, act = nc.sync, nc.vector, nc.scalar
        tl = list(tts if tts is not None else range(NT))
        yslots = [self.s["scr32"][:, 2048:4096], self.s["scr32"][:, 6144:8192]]
        xSB = self.xSB
        yev = {}
        yused = [None, None]
        xused = None

        def load_y(i):
            tt = tl[i]
            yev[i] = self.dma_h(yslots[i % 2],
                                y_dram[tt * 128:(tt + 1) * 128, :], "X",
                                yused[i % 2], self.ev("X") if i == 0 else None)

        if y_dram is not None:
            load_y(0)
        for i, tt in enumerate(tl):
            if y_dram is not None and i + 1 < len(tl):
                load_y(i + 1)
            ex = self.dma_h(xSB, x_src[tt * 128:(tt + 1) * 128, :], "X", xused,
                            self.ev("X") if (i == 0 and y_dram is None) else None)
            if y_dram is not None:
                ysrc = yslots[i % 2]
                self.wait(vec, yev[i])
                self.wait(act, yev[i])
            else:
                ysrc = y_tiles(tt)
            self.ln_tile(ysrc, self.xnSB)
            if y_dram is not None:
                yused[i % 2] = self.ev("A")
            self.wait(vec, ex)
            self.inc(vec.tensor_add(xSB, xSB, self.xnSB), "V")
            xused = self.dma(nc.gpsimd, x_dst[tt * 128:(tt + 1) * 128, :], xSB, "X",
                             self.ev("V"))

    def l0_attn(self):
        nc, s, d = self.nc, self.s, self.d
        sync, gps, ten, vec, act = nc.sync, nc.gpsimd, nc.tensor, nc.vector, nc.scalar
        xnT, o_all, scrb = s["xnT"], s["bbuf"], s["scrb"]
        cosF = s["constb"][:, 0:2048]
        sinF = s["constb"][:, 2048:4096]
        ms = [s["constb"][:, 4096 + m * 512:4096 + (m + 1) * 512] for m in range(4)]
        qT = scrb[:, 0:2048]
        kT = scrb[:, 2048:4096]
        vS = scrb[:, 4096:6144]
        expS = [scrb[:, 6144:6656], scrb[:, 6656:7168]]
        ws = [scrb[:, 7168:9216], scrb[:, 9216:11264]]
        pt, pb4, pb5, ptf = s["pt"], s["pb4"], s["pb5"], s["ptf"]
        stats = s["stats"]
        srow = s["srow"]
        wsrc = [d["L0_attn_w_aq"], d["L0_attn_w_ak"], d["L0_attn_w_av"]]
        slot_free = [None, None]
        load_ev = {}

        def load_w(h, which):
            slot = (3 * h + which) % 2
            src = wsrc[which][h].rearrange("(j p) d -> p j d", p=128)
            dst = ws[slot].rearrange("p (j d) -> p j d", j=ND)
            load_ev[(h, which)] = (slot, self.dma(gps, dst, src, "W",
                                                  slot_free[slot]))

        def proj_rope(h, which, dstT, amax_col):
            slot, wev = load_ev[(h, which)]
            self.wait(ten, wev, self.ev("A"))
            for j in range(ND):
                for c in range(4):
                    pe = ten.matmul(pt[:, c * 512:(c + 1) * 512],
                                    ws[slot][:, j * 128:(j + 1) * 128],
                                    xnT[:, j * 2048 + c * 512: j * 2048 + (c + 1) * 512],
                                    start=(j == 0), stop=(j == ND - 1))
            pev = self.inc(pe, "P")
            slot_free[slot] = pev
            self.wait(act, pev, self.ev("V"))
            for c in range(4):
                self.A(act.copy(vS[:, c * 512:(c + 1) * 512],
                                  pt[:, c * 512:(c + 1) * 512]))
            self.wait(vec, self.ev("A"))
            for c in range(4):
                t1, t2 = expS[0], expS[1]
                cc = slice(c * 512, (c + 1) * 512)
                self.V(vec.tensor_tensor(t1, vS[:, cc], cosF[:, cc], ALU.mult))
                self.V(vec.tensor_tensor(t2[0:64, :], vS[64:128, cc], sinF[64:128, cc],
                                           ALU.mult))
                self.V(vec.tensor_tensor(t2[64:128, :], vS[0:64, cc], sinF[0:64, cc],
                                           ALU.mult))
                self.V(vec.tensor_add(dstT[:, cc], t1, t2))
            self.V(vec.reduce_max(amax_col, dstT, axis=AX.X,
                                    apply_absolute_value=True))
            return self.ev("V")

        load_w(0, 0)
        load_w(0, 1)
        last_ydma = None
        for h in range(NH):
            evq = proj_rope(h, 0, qT, stats[:, 3:4])
            load_w(h, 2)
            evk = proj_rope(h, 1, kT, stats[:, 4:5])
            self.wait(ten, evq, evk)
            for i in range(2):
                pe = self.inc(ten.transpose(ptf[0:1, 0:128], stats[:, 3 + i:4 + i],
                                            s["ident32"][:, :]), "P")
                self.wait(vec, pe)
                ve = self.V(vec.reduce_max(srow[0:1, i:i + 1], ptf[0:1, 0:128],
                                             axis=AX.X))
                self.wait(ten, ve)
            self.V(vec.tensor_tensor(srow[0:1, 2:3], srow[0:1, 0:1], srow[0:1, 1:2],
                                       ALU.mult))
            self.V(vec.tensor_scalar(srow[0:1, 2:3], srow[0:1, 2:3], -1.0, None,
                                       ALU.mult))
            self.bcast_col(srow[0:1, 2:3], stats[:, 7:8])
            negG = stats[:, 7:8]
            slot, wev = load_ev[(h, 2)]
            self.wait(ten, wev)
            for tt in range(NT):
                bank = pt[:, (tt // 4) * 512 + (tt % 4) * 128:
                          (tt // 4) * 512 + (tt % 4) * 128 + 128]
                for j in range(ND):
                    pe = ten.matmul(bank,
                                    xnT[:, j * 2048 + tt * 128: j * 2048 + tt * 128 + 128],
                                    ws[slot][:, j * 128:(j + 1) * 128],
                                    start=(j == 0), stop=(j == ND - 1))
            pev = self.inc(pe, "P")
            slot_free[slot] = pev
            self.wait(act, pev, self.ev("V"))
            for c in range(4):
                self.A(act.copy(vS[:, c * 512:(c + 1) * 512],
                                  pt[:, c * 512:(c + 1) * 512]))
            self.wait(ten, self.ev("A"))
            if h + 1 < NH:
                load_w(h + 1, 0)
                load_w(h + 1, 1)
            scale_ve = None
            for ic in range(4):
                njt = min(NT, (ic + 1) * 4)
                expev = [scale_ve, None]
                sev = [None, None]
                exprd = [None, None]

                def s_mm(jt):
                    sbank = (pb4, pb5)[jt % 2]
                    self.wait(ten, exprd[jt % 2])
                    sev[jt % 2] = self.inc(
                        ten.matmul(sbank[:, :], kT[:, jt * 128:(jt + 1) * 128],
                                   qT[:, ic * 512:(ic + 1) * 512],
                                   start=True, stop=True), "P")

                s_mm(0)
                for jt in range(njt):
                    sbank = (pb4, pb5)[jt % 2]
                    self.wait(act, sev[jt % 2], expev[jt % 2])
                    ae = self.A(act.activation(expS[jt % 2], sbank[:, :], AF.Exp,
                                               bias=negG, scale=1.0 / DH))
                    exprd[jt % 2] = ae
                    ready = ae
                    off = ic * 512 - jt * 128
                    if -384 <= off <= 0:
                        mi = (-off) // 128
                        self.wait(vec, ae)
                        ready = self.V(vec.tensor_tensor(expS[jt % 2], expS[jt % 2],
                                                         ms[mi], ALU.mult))
                    if jt + 1 < njt:
                        s_mm(jt + 1)
                    self.wait(ten, ready)
                    pe = ten.matmul(pt[:, 0:512], vS[:, jt * 128:(jt + 1) * 128],
                                    expS[jt % 2], start=(jt == 0), stop=(jt == njt - 1))
                    pe = ten.matmul(pt[0:1, 512:1024], s["onescol"][:, :], expS[jt % 2],
                                    start=(jt == 0), stop=(jt == njt - 1))
                    expev[jt % 2] = self.inc(pe, "P")
                pev = self.ev("P")
                self.wait(vec, pev)
                ve = self.V(vec.reciprocal(s["srow"][0:1, 0:512], pt[0:1, 512:1024]))
                self.wait(ten, ve)
                pe = self.inc(ten.matmul(pt[:, 1024:1536], s["ones32"][:, :],
                                         s["srow"][0:1, 0:512], start=True, stop=True), "P")
                self.wait(act, pe)
                ae = self.A(act.copy(expS[0], pt[:, 1024:1536]))
                self.wait(vec, ae)
                ve = self.V(vec.tensor_tensor(
                    o_all[:, h * 2048 + ic * 512: h * 2048 + (ic + 1) * 512],
                    pt[:, 0:512], expS[0], ALU.mult))
                scale_ve = ve
                self.wait(ten, ve)
        wstage = scrb[:, 0:8192]
        ystage = self.ySB
        for oc in range(4):
            evs = []
            for h in range(NH):
                evs.append(self.dma(gps, wstage[:, h * 512:(h + 1) * 512],
                                    d["L0_attn_w_ao"][h][:, oc * 512:(oc + 1) * 512],
                                    "W", self.ev("P") if (oc == 0 and h == 0) else None))
            self.wait(ten, *evs)
            for tt in range(NT):
                bank = (pb4, pb5)[tt % 2]
                self.wait(ten, self.ev("A"))
                for hh in range(NH):
                    pe = ten.matmul(bank[:, :],
                                    o_all[:, hh * 2048 + tt * 128: hh * 2048 + tt * 128 + 128],
                                    wstage[:, hh * 512:(hh + 1) * 512],
                                    start=(hh == 0), stop=(hh == NH - 1))
                pe = self.inc(pe, "P")
                self.wait(act, pe, last_ydma)
                ae = self.A(act.copy(ystage[:, 0:512], bank[:, :]))
                last_ydma = self.dma(
                    gps, d["ybuf"][tt * 128:(tt + 1) * 128, oc * 512:(oc + 1) * 512],
                    ystage[:, 0:512], "X", ae)

    def l0_ffn(self):
        nc, s, d = self.nc, self.s, self.d
        sync, gps, ten, vec, act = nc.sync, nc.gpsimd, nc.tensor, nc.vector, nc.scalar
        xnT, baseWT, scrb = s["xnT"], s["bbuf"], s["scrb"]
        pt, pb4, pb5 = s["pt"], s["pb4"], s["pb5"]
        A_all = scrb[:, 0:2048]
        B_all = scrb[:, 2048:4096]
        gxs = scrb[:, 4096:6144]
        gmid = scrb[:, 6144:8192]
        g0Tb = s["g0T"]
        ec = self.dma(gps, d["wscr0"], d["L0_ffn_base_w"], "W", self.ev("P"))
        tev = []
        for j in range(ND):
            tev.append(self.dma_t(baseWT[:, j * 2048:(j + 1) * 2048],
                                  d["wscr0"][:, j * 128:(j + 1) * 128], ec))
        aev = []
        for e in range(E):
            src = d["L0_ffn_lora_A"][e].rearrange("(j p) r -> p j r", p=128)
            dst = A_all.rearrange("p (j er) -> p j er", j=ND)[:, :, e * R:(e + 1) * R]
            aev.append(self.dma(gps, dst, src, "W"))
        bev = self.dma(gps, B_all, d["L0_ffn_lora_B"].rearrange("e r d -> (e r) d"), "W")
        ebb = self.dma(gps, s["brow"][:, :], d["L0_ffn_base_b"].rearrange("(a d) -> a d", a=1), "W")
        self.wait(ten, bev, ebb, *aev)
        ystage = self.ySB
        for c in range(4):
            cc = slice(c * 512, (c + 1) * 512)
            pe = self.inc(ten.matmul(pb4[:, :], s["e16b"][:, :], g0Tb[0:8, cc],
                                     start=True, stop=True), "P")
            self.wait(act, pe)
            ae = self.A(act.copy(gxs[:, cc], pb4[:, :]))
            self.wait(ten, self.ev("A"))
            for j in range(ND):
                pe = ten.matmul(pb5[:, :], A_all[:, j * 128:(j + 1) * 128],
                                xnT[:, j * 2048 + c * 512: j * 2048 + (c + 1) * 512],
                                start=(j == 0), stop=(j == ND - 1))
            pe = self.inc(pe, "P")
            self.wait(vec, pe, ae)
            ve = self.V(vec.tensor_tensor(gmid[:, cc], pb5[:, :], gxs[:, cc],
                                            ALU.mult))
            self.wait(ten, ve)
        self.wait(ten, *tev)
        last_ydma = None
        evacb = [None, None, None, None]
        for tt in range(NT):
            for oc in range(4):
                bank = pt[:, oc * 512:(oc + 1) * 512]
                self.wait(ten, evacb[oc])
                for j in range(ND):
                    pe = ten.matmul(bank,
                                    xnT[:, j * 2048 + tt * 128: j * 2048 + tt * 128 + 128],
                                    baseWT[:, j * 2048 + oc * 512: j * 2048 + (oc + 1) * 512],
                                    start=(j == 0), stop=False)
                pe = ten.matmul(bank, s["onesrow"][:, :], s["brow"][0:1, oc * 512:(oc + 1) * 512],
                                start=False, stop=False)
                pe = ten.matmul(bank, gmid[:, tt * 128:(tt + 1) * 128],
                                B_all[:, oc * 512:(oc + 1) * 512], start=False, stop=True)
                pe = self.inc(pe, "P")
                self.wait(act, pe, last_ydma)
                evacb[oc] = self.A(act.copy(ystage[:, oc * 512:(oc + 1) * 512], bank))
            last_ydma = self.dma(sync, d["ybuf"][tt * 128:(tt + 1) * 128, :], ystage,
                                 "X", self.ev("A"))

    def l1_attn(self):
        nc, s, d = self.nc, self.s, self.d
        sync, gps, ten, vec, act = nc.sync, nc.gpsimd, nc.tensor, nc.vector, nc.scalar
        xnT, xnRow, scrb = s["xnT"], s["bbuf"], s["scrb"]
        pt, pb4 = s["pt"], s["pb4"]
        aT = s["scr32"][0:64, 0:2048]
        bT = s["scr32"][0:64, 2048:4096]
        hsT = s["scr32"][0:64, 4096:6144]
        WaTb = scrb[:, 0:1024]
        WbTb = scrb[:, 1024:2048]
        WoutTb = scrb[0:64, 2048:4096]
        hsb = scrb[0:64, 4096:6144]
        bcol = s["stats"][0:64, 8:10]
        wtmp = s["scr32"][0:64, 0:2048]
        for (wname, dstT) in (("L1_attn_Wa", WaTb), ("L1_attn_Wb", WbTb)):
            ew = self.dma(gps, wtmp, d[wname], "W", self.ev("P"), self.ev("V"),
                          self.ev("A"))
            self.wait(ten, ew)
            for j in range(ND):
                pe = self.inc(ten.transpose(s["ptf"][:, 0:64],
                                            wtmp[0:64, j * 128:(j + 1) * 128],
                                            s["ident32"][0:64, 0:64]), "P")
                self.wait(act, pe)
                ae = self.A(act.copy(dstT[:, j * 64:(j + 1) * 64], s["ptf"][:, 0:64]))
                self.wait(ten, ae)
        wotmp = s["scr32"][:, 2048:3072]
        ew = self.dma(gps, wotmp.rearrange("p (j c) -> p j c", j=ND),
                      d["L1_attn_Wout"].rearrange("(j p) c -> p j c", p=128), "W")
        self.wait(ten, ew)
        for j in range(ND):
            pe = self.inc(ten.transpose(s["ptf"][0:64, 0:128], wotmp[:, j * 64:(j + 1) * 64],
                                        s["ident32"][:, :]), "P")
            self.wait(act, pe)
            ae = self.A(act.copy(WoutTb[:, j * 128:(j + 1) * 128], s["ptf"][0:64, 0:128]))
            self.wait(ten, ae)
        eba = self.dma(sync, bcol[:, 0:1], d["L1_attn_ba"].rearrange("(p a) -> p a", a=1), "X")
        ebb = self.dma(sync, bcol[:, 1:2], d["L1_attn_bb"].rearrange("(p a) -> p a", a=1), "X")
        ebo = self.dma(gps, s["brow"][:, :], d["L1_attn_bout"].rearrange("(a c) -> a c", a=1), "W")
        self.wait(act, eba, ebb)
        for (wT, bias, dstT) in ((WaTb, bcol[:, 0:1], aT), (WbTb, bcol[:, 1:2], bT)):
            for c in range(4):
                for j in range(ND):
                    pe = ten.matmul(pb4[0:64, :], wT[:, j * 64:(j + 1) * 64],
                                    xnT[:, j * 2048 + c * 512: j * 2048 + (c + 1) * 512],
                                    start=(j == 0), stop=(j == ND - 1))
                pe = self.inc(pe, "P")
                self.wait(act, pe)
                ae = self.A(act.activation(dstT[:, c * 512:(c + 1) * 512], pb4[0:64, :],
                                             AF.Tanh, bias=bias, scale=1.0))
                self.wait(ten, ae)
        self.wait(vec, self.ev("A"))
        self.V(vec.tensor_tensor_scan(hsT, aT, bT, 0.0, ALU.mult, ALU.add))
        sv = self.V(vec.tensor_copy(hsb, hsT))
        self.wait(ten, sv, ebo)
        ystage = self.ySB
        last_ydma = None
        for tt in range(NT):
            self.wait(ten, self.ev("V"))
            for dc in range(4):
                bank = pt[:, dc * 512:(dc + 1) * 512]
                pe = ten.matmul(bank, hsb[:, tt * 128:(tt + 1) * 128],
                                WoutTb[:, dc * 512:(dc + 1) * 512], start=True, stop=False)
                pe = ten.matmul(bank, s["onesrow"][:, :], s["brow"][0:1, dc * 512:(dc + 1) * 512],
                                start=False, stop=True)
                self.inc(pe, "P")
            pev = self.ev("P")
            self.wait(vec, pev, last_ydma)
            ve = self.V(vec.tensor_add(ystage, pt[:, :],
                                         xnRow[:, tt * 2048:(tt + 1) * 2048]))
            last_ydma = self.dma(sync, d["ybuf"][tt * 128:(tt + 1) * 128, :], ystage,
                                 "X", ve)

    def l1_route_finish(self):
        nc, s, d = self.nc, self.s, self.d
        sync, gps, ten, vec, act = nc.sync, nc.gpsimd, nc.tensor, nc.vector, nc.scalar
        oh1 = s["r_oh1"][0:8, :]
        oh2 = s["r_oh2"][0:8, :]
        Ssum = s["r_S"][0:8, :]
        sel = s["r_sel"][0:8, :]
        incl = s["r_incl"][0:8, :]
        ones8 = s["r_ones"][0:8, :]
        g = s["gates"]
        self.V(vec.tensor_add(Ssum, oh1, oh2))
        self.V(vec.memset(ones8, 1.0))
        sv = self.V(vec.tensor_tensor_scan(incl, ones8, Ssum, 0.0, ALU.mult, ALU.add))
        cdma = self.dma(sync, d["cc_in"], incl[:, T - 1:T], "X", sv)
        self.wait(gps, cdma)
        cc = self.inc(gps.collective_compute(
            "AllGather", ALU.bypass, replica_groups=[list(range(NCORES))],
            ins=[d["cc_in"]], outs=[d["cc_out"]], unique_tensors="No"), "C", 1)
        g88 = g[0:8, GC_G88:GC_G88 + 8]
        mprev = g[0:8, GC_MPREV:GC_MPREV + 8]
        emp = self.dma(sync, mprev, d["c_maskprev"], "X")
        edg = self.dma(sync, g88, d["cc_out"].rearrange("(r e) a -> r (e a)", e=8), "X", cc)
        self.wait(ten, edg)
        pe = self.inc(ten.transpose(s["ptf"][0:8, 0:8], g88, s["ident32"][0:8, 0:8]), "P")
        self.wait(vec, pe, emp)
        self.V(vec.tensor_tensor(g[0:8, GC_MSK:GC_MSK + 8], s["ptf"][0:8, 0:8], mprev,
                                   ALU.mult))
        off = s["stats"][0:8, 8:9]
        ve = self.V(vec.reduce_sum(off, g[0:8, GC_MSK:GC_MSK + 8], axis=AX.X))
        self.wait(ten, ve)
        for k, oh in ((0, oh1), (1, oh2)):
            sub = Ssum if k == 0 else oh2
            self.V(vec.tensor_tensor(sel, incl, sub, ALU.subtract))
            self.V(vec.tensor_scalar(sel, sel, off, None, ALU.add))
            self.V(vec.tensor_tensor(sel, sel, oh, ALU.mult))
            for tt in range(NT):
                self.wait(ten, self.ev("V"))
                pe = self.inc(ten.transpose(s["ptf"][:, 0:8],
                                            sel[:, tt * 128:(tt + 1) * 128],
                                            s["ident32"][0:8, 0:8]), "P")
                self.wait(vec, pe)
                pos = s["stats"][:, 10:11]
                self.V(vec.reduce_sum(pos, s["ptf"][:, 0:8], axis=AX.X))
                val = s["stats"][:, 11:12]
                self.V(vec.tensor_scalar(val, pos, float(CAP), None, ALU.is_lt))
                cwk = g[:, GC_CW + tt * 2 + k: GC_CW + tt * 2 + k + 1]
                wk = s["stats"][:, 12 + k:13 + k]
                ve = self.V(vec.tensor_tensor(wk, cwk, val, ALU.mult))
                self.wait(ten, ve)
                eqbase = GC_EQ1 if k == 0 else GC_EQ2
                eq = g[:, eqbase + tt * 8:eqbase + tt * 8 + 8]
                W8 = g[:, tt * 8:(tt + 1) * 8]
                if k == 0:
                    self.V(vec.tensor_scalar(W8, eq, wk, None, ALU.mult))
                else:
                    self.V(vec.scalar_tensor_tensor(W8, eq, wk, W8, ALU.mult, ALU.add))

    def l1_ffn_a(self):
        nc, s, d = self.nc, self.s, self.d
        sync, gps, ten, vec, act = nc.sync, nc.gpsimd, nc.tensor, nc.vector, nc.scalar
        xnT = s["xnT"]
        pairs = [(s["pb4"], s["pb5"]), (s["ptf"], s["pb7"])]
        wst = [s["pascr"][:, 0:4096], s["pascr"][:, 4096:8192]]
        hrow = [s["pascr"][:, 8192:10240], s["pascr"][:, 10240:12288]]
        sg = [s["scr32"][:, 0:512], s["scr32"][:, 512:1024]]
        items = []
        for ei in range(E + 1):
            shared = (ei == E)
            nj = NJS if shared else NJ1
            for j in range(nj):
                if shared:
                    gsrc = d["L1_ffn_shared_w_up_gate"][:, j * 128:(j + 1) * 128]
                    usrc = d["L1_ffn_shared_w_up_gate"][:, J1 + j * 128: J1 + (j + 1) * 128]
                    basej = E * NJ1 * 128
                else:
                    gsrc = d["L1_ffn_w_up_gate"][ei][:, j * 128:(j + 1) * 128]
                    usrc = d["L1_ffn_w_up_gate"][ei][:, I1 + j * 128: I1 + (j + 1) * 128]
                    basej = ei * NJ1 * 128
                items.append((gsrc, usrc, basej + j * 128))
        wfree = [None, None]
        wev = {}

        def issue(k):
            slot = k % 2
            gsrc, usrc, _ = items[k]
            e1 = self.dma(gps, wst[slot][:, 0:2048].rearrange("p (k c) -> p k c", k=ND),
                          gsrc.rearrange("(k p) c -> p k c", p=128), "W", wfree[slot])
            e2 = self.dma(gps, wst[slot][:, 2048:4096].rearrange("p (k c) -> p k c", k=ND),
                          usrc.rearrange("(k p) c -> p k c", p=128), "W")
            wev[k] = (e1, e2)

        issue(0)
        issue(1)
        self.wait(self.nc.tensor, self.ev("V"), self.ev("A"))
        evac_done = [None, None]
        hdma = [None, None]
        for k, (gsrc, usrc, hbase) in enumerate(items):
            slot = k % 2
            hslot = k % 2
            self.wait(ten, *wev[k])
            pev = None
            for c in range(4):
                par = c % 2
                self.wait(ten, evac_done[par])
                for jj in range(ND):
                    ten.matmul(pairs[par][0][:, :], wst[slot][:, jj * 128:(jj + 1) * 128],
                               xnT[:, jj * 2048 + c * 512: jj * 2048 + (c + 1) * 512],
                               start=(jj == 0), stop=(jj == ND - 1))
                    pu = ten.matmul(pairs[par][1][:, :],
                                    wst[slot][:, 2048 + jj * 128: 2048 + (jj + 1) * 128],
                                    xnT[:, jj * 2048 + c * 512: jj * 2048 + (c + 1) * 512],
                                    start=(jj == 0), stop=(jj == ND - 1))
                pev = self.inc(pu, "P")
                self.wait(act, pev)
                ae = self.A(act.activation(sg[par], pairs[par][0][:, :], AF.Silu))
                self.wait(vec, ae, hdma[hslot] if c == 0 else None)
                ve = self.V(vec.tensor_tensor(hrow[hslot][:, c * 512:(c + 1) * 512],
                                              sg[par], pairs[par][1][:, :], ALU.mult))
                evac_done[par] = ve
            wfree[slot] = pev
            if k + 2 < len(items):
                issue(k + 2)
            hdma[hslot] = self.dma(gps, d["hbuf"][hbase: hbase + 128, :],
                                   hrow[hslot], "X", self.ev("V"))

    def l1_ffn_b(self):
        nc, s, d = self.nc, self.s, self.d
        sync, gps, ten, vec, act = nc.sync, nc.gpsimd, nc.tensor, nc.vector, nc.scalar
        pt = s["pt"]
        yacc = s["yacc"]
        wdB = [s["wdB"], s["wdB2"]]
        hst = [s["hst"][:, 0:1408], s["hst"][:, 1408:2816]]
        g = s["gates"]
        xsrc = d["xres3"] if self.fused else d["x"]
        self.hx_ev = self.ev("X")
        groups = [("r", e) for e in range(E)] + [("s", gi) for gi in range(4)]
        ve = self.V(vec.memset(yacc[:, :], 0.0))
        fold_ev = [None, None, None, None]
        wfree = [None, None]
        hslot_free = [None, None]
        wev = {}

        def wsrc_of(kind, gi):
            if kind == "r":
                return d["L1_ffn_w_down"][gi], gi * NJ1 * 128
            return d["L1_ffn_shared_w_down"][gi * I1:(gi + 1) * I1, :], (E + gi) * NJ1 * 128

        def issue_w(gk, kind, gi):
            slot = gk % 2
            wsrc, _ = wsrc_of(kind, gi)
            wev[gk] = self.dma(gps, wdB[slot][:, :].rearrange("p (j c) -> p j c", j=NJ1),
                               wsrc.rearrange("(j p) c -> p j c", p=128), "W", wfree[slot])

        for half in range(2):
            tts = list(range(half * 8, half * 8 + 8))
            self.wait(ten, self.ev("V"), self.ev("A"))
            for gidx, (kind, gi) in enumerate(groups):
                gk = half * 12 + gidx
                if gk == 0:
                    issue_w(0, kind, gi)
                if gk + 1 < 24:
                    nk, ngi = groups[(gk + 1) % 12]
                    issue_w(gk + 1, nk, ngi)
                slot = gk % 2
                _, basej = wsrc_of(kind, gi)

                def issue_h(ti):
                    tt = tts[ti]
                    return self.dma(sync, hst[ti % 2].rearrange("p (j c) -> p j c", j=NJ1),
                                    d["hbuf"][basej:basej + NJ1 * 128,
                                              tt * 128:(tt + 1) * 128].rearrange(
                                        "(j p) c -> p j c", p=128),
                                    "X", hslot_free[ti % 2], self.hx_ev)

                hready = [None] * 8
                hready[0] = issue_h(0)
                for ti, tt in enumerate(tts):
                    if ti + 1 < 8:
                        hready[ti + 1] = issue_h(ti + 1)
                    self.wait(ten, hready[ti], wev[gk])
                    for dc in range(4):
                        bank = pt[:, dc * 512:(dc + 1) * 512]
                        self.wait(ten, fold_ev[dc])
                        for j in range(NJ1):
                            pe = ten.matmul(bank, hst[ti % 2][:, j * 128:(j + 1) * 128],
                                            wdB[slot][:, j * 2048 + dc * 512: j * 2048 + (dc + 1) * 512],
                                            start=(j == 0), stop=(j == NJ1 - 1))
                        pe = self.inc(pe, "P")
                        self.wait(vec, pe)
                        ys = yacc[:, ti * 2048 + dc * 512: ti * 2048 + (dc + 1) * 512]
                        if kind == "r":
                            wcol = g[:, tt * 8 + gi: tt * 8 + gi + 1]
                            ve = self.V(vec.scalar_tensor_tensor(
                                ys, bank, wcol, ys, ALU.mult, ALU.add))
                        else:
                            ve = self.V(vec.tensor_add(ys, ys, bank))
                        fold_ev[dc] = ve
                    hslot_free[ti % 2] = self.ev("P")
                wfree[slot] = self.ev("P")
            self.post_ln(xsrc, d["xo"],
                         y_tiles=(lambda h: (lambda tt: yacc[:, (tt - h * 8) * 2048:
                                                             (tt - h * 8 + 1) * 2048]))(half),
                         tts=tts)
            if half == 0:
                self.wait(vec, self.ev("X"))
                self.V(vec.memset(yacc[:, :], 0.0))

    def emit(self):
        nc, s, d = self.nc, self.s, self.d
        self.load_consts()
        layers = self.layers
        fused = self.fused
        xsrc = d["x"]
        with ExitStack() as big:
            s["xnT"] = big.enter_context(nc.sbuf_tensor("xnT", [128, ND * 2048], BF16))
            if 0 in layers or 1 in layers or 2 in layers:
                with ExitStack() as mid:
                    s["bbuf"] = mid.enter_context(
                        nc.sbuf_tensor("bbuf", [128, ND * 2048], BF16))
                    s["scrb"] = mid.enter_context(
                        nc.sbuf_tensor("scrb", [128, 11264], BF16))
                    s["constb"] = mid.enter_context(
                        nc.sbuf_tensor("constb", [128, 6144], BF16))
                    if 0 in layers:
                        evs = [self.dma(nc.sync, s["constb"][:, 0:2048], d["c_cosF"], "X"),
                               self.dma(nc.sync, s["constb"][:, 2048:4096], d["c_sinF"], "X")]
                        for m in range(4):
                            evs.append(self.dma(
                                nc.sync,
                                s["constb"][:, 4096 + m * 512:4096 + (m + 1) * 512],
                                d["c_masks"][m * 128:(m + 1) * 128, :], "X"))
                        for eng in (nc.tensor, nc.vector, nc.scalar):
                            self.wait(eng, *evs)
                    if 1 in layers:
                        s["g0T"] = mid.enter_context(nc.sbuf_tensor("g0T", [8, 2048], BF16))
                        s["e16b"] = s["constb"][0:8, 0:128]
                    if 0 in layers:
                        self.pre_ln(xsrc)
                        self.l0_attn()
                        dst = d["xres1"] if fused else d["xo"]
                        self.post_ln(xsrc, dst, y_dram=d["ybuf"])
                        xsrc = dst
                    if 1 in layers:
                        ee = self.dma(nc.sync, s["e16b"], d["c_e16"], "X",
                                      self.ev("P"), self.ev("V"), self.ev("A"))
                        self.wait(nc.tensor, ee)
                        self.pre_ln(xsrc, router=0)
                        self.l0_ffn()
                        dst = d["xres2"] if fused else d["xo"]
                        self.post_ln(xsrc, dst, y_dram=d["ybuf"])
                        xsrc = dst
                    if 2 in layers:
                        self.pre_ln(xsrc, xnRow=s["bbuf"])
                        self.l1_attn()
                        dst = d["xres3"] if fused else d["xo"]
                        self.post_ln(xsrc, dst, y_dram=d["ybuf"])
                        xsrc = dst
            if 3 in layers:
                with ExitStack() as mid:
                    for rn in ("r_oh1", "r_oh2", "r_S", "r_sel", "r_incl", "r_ones"):
                        s[rn] = mid.enter_context(
                            nc.sbuf_tensor(rn, [8, 2048], F32))
                    self.pre_ln(xsrc, router=1)
                    self.l1_route_finish()
                    with ExitStack() as inner:
                        s["pascr"] = inner.enter_context(
                            nc.sbuf_tensor("pascr", [128, 12288], BF16))
                        self.l1_ffn_a()
        if 3 in layers:
            with ExitStack() as mid:
                s["yacc"] = mid.enter_context(nc.sbuf_tensor("yacc", [128, 8 * 2048], F32))
                s["wdB"] = mid.enter_context(nc.sbuf_tensor("wdB", [128, NJ1 * 2048], BF16))
                s["wdB2"] = mid.enter_context(nc.sbuf_tensor("wdB2", [128, NJ1 * 2048], BF16))
                s["hst"] = mid.enter_context(nc.sbuf_tensor("hst", [128, 2 * 1408], BF16))
                self.l1_ffn_b()
        nc = self.nc
        nc.vector.wait_ge(self.sem["V"], self.cnt["V"])
        nc.scalar.wait_ge(self.sem["A"], self.cnt["A"])
        nc.tensor.wait_ge(self.sem["P"], self.cnt["P"])
        nc.gpsimd.wait_ge(self.sem["W"], self.cnt["W"])
        if self.cnt["C"]:
            nc.gpsimd.wait_ge(self.sem["C"], self.cnt["C"])
        nc.sync.wait_ge(self.sem["X"], self.cnt["X"])
        nc.all_engine_barrier()
        self.ctx.close()
        return self.nc


_CONSTS = _consts()


def _in_maps(layers, inputs, xs):
    maps = []
    need = {"c_ident32", "c_identb", "c_onesrow", "c_onescol", "c_ones32"}
    wnames = []
    if 0 in layers:
        need |= {"c_cosF", "c_sinF", "c_masks"}
        wnames += ["L0_attn_w_aq", "L0_attn_w_ak", "L0_attn_w_av", "L0_attn_w_ao"]
    if 1 in layers:
        need |= {"c_e16"}
        wnames += ["L0_ffn_base_w", "L0_ffn_base_b", "L0_ffn_router_w",
                   "L0_ffn_router_b", "L0_ffn_lora_A", "L0_ffn_lora_B"]
    if 2 in layers:
        wnames += ["L1_attn_Wa", "L1_attn_ba", "L1_attn_Wb", "L1_attn_bb",
                   "L1_attn_Wout", "L1_attn_bout"]
    if 3 in layers:
        wnames += ["L1_ffn_router", "L1_ffn_w_up_gate", "L1_ffn_w_down",
                   "L1_ffn_shared_w_up_gate", "L1_ffn_shared_w_down"]
    for c in range(NCORES):
        m = {"x": np.ascontiguousarray(xs[c])}
        for n in need:
            m[n] = _CONSTS[n]
        for n in wnames:
            m[n] = inputs[n]
        if 3 in layers:
            mp = np.zeros((8, 8), np.float32)
            mp[:, :c] = 1.0
            m["c_maskprev"] = mp
        maps.append(m)
    return maps


def kernel(**inputs):
    inputs = {k: np.asarray(v) for k, v in inputs.items()}
    x = inputs["L0_attn_x"]
    xs = [np.asarray(x[c]) for c in range(NCORES)]
    layers = [0, 1, 2, 3]
    nc = P(layers, fused=True).emit()
    maps = _in_maps(layers, inputs, xs)
    res = run_bass_kernel_spmd(nc, maps, core_ids=list(range(NCORES)))
    return np.stack([res.results[c]["xo"] for c in range(NCORES)], axis=0)
